# version 07
# baseline (speedup 1.0000x reference)
import math
from contextlib import ExitStack
import numpy as np
import ml_dtypes
import concourse.bass as bass
import concourse.bacc as bacc
import concourse.mybir as mybir
from concourse.bass_utils import run_bass_kernel_spmd

F32 = mybir.dt.float32
BF16 = mybir.dt.bfloat16
I32 = mybir.dt.int32
AF = mybir.ActivationFunctionType
ALU = mybir.AluOpType
AX = mybir.AxisListType
BF = ml_dtypes.bfloat16

NCORES = 8
SEQ = 2048
NSEQ = 2
DM = 1024
DC = DM // 128
TT = 512
NTT = SEQ // TT
EPS = 1e-5


class Sem:
    def __init__(self, nc, name):
        self.h = nc.alloc_semaphore(name)
        self.count = 0

    def inc(self, instr, n=1):
        instr.then_inc(self.h, n)
        self.count += n
        return self.count

    def wait(self, eng, count=None):
        c = self.count if count is None else count
        if c > 0:
            eng.wait_ge(self.h, c)


class DmaQ:

    def __init__(self, nc, eng, name, nlanes=6):
        self.eng = eng
        self.lanes = [Sem(nc, f"{name}{i}") for i in range(nlanes)]
        self.i = 0

    def put(self, out, in_, **kw):
        lane = self.lanes[self.i % len(self.lanes)]
        self.i += 1
        ins = self.eng.dma_start(out=out, in_=in_, **kw)
        cnt = lane.inc(ins, 16)
        return (lane, cnt)

    def wait(self, eng, tok):
        lane, _ = tok
        eng.wait_ge(lane.h, lane.count)

    def wait_all(self, eng):
        for lane in self.lanes:
            lane.wait(eng)


class Core:
    def __init__(self, nc):
        self.nc = nc
        self.t = nc.tensor
        self.s = nc.scalar
        self.v = nc.vector
        self.g = nc.gpsimd
        self.y = nc.sync
        self.ld = DmaQ(nc, self.y, "ld", 6)
        self.wq = DmaQ(nc, self.g, "wq", 6)
        self.mm = Sem(nc, "mm")
        self.ev = Sem(nc, "ev")
        self.dv = Sem(nc, "dv")
        self.pl = Sem(nc, "pl")
        self.jn = Sem(nc, "jn")
        self._jk = 0
        self._uid = 0

    def u(self, name):
        self._uid += 1
        return f"{name}_{self._uid}"

    def engines(self):
        return [self.t, self.s, self.v, self.g, self.y]

    def vop(self, f, *a, **k):
        self.dv.wait(self.v)
        ins = f(*a, **k)
        return self.dv.inc(ins)

    def sop(self, f, *a, **k):
        self.ev.wait(self.s)
        ins = f(*a, **k)
        return self.ev.inc(ins)

    def gop(self, f, *a, **k):
        self.pl.wait(self.g)
        ins = f(*a, **k)
        return self.pl.inc(ins)

    def join(self):
        self.ld.wait_all(self.y)
        self.wq.wait_all(self.g)
        self.ev.wait(self.s)
        self.dv.wait(self.v)
        self.pl.wait(self.g)
        self._jk += 1
        for e in self.engines():
            e.nop().then_inc(self.jn.h, 1)
        self.jn.count += 5
        for e in self.engines():
            e.wait_ge(self.jn.h, self.jn.count)


def xap(X, si, tt, ntok=TT):
    return X[si, :, :, tt * ntok:(tt + 1) * ntok].rearrange("c p n -> p c n")


def emit_ln_tile(co, C, xt, xn_out, ps_st, ps_bc, xb, sq, rows, n):
    t, s, v = co.t, co.s, co.v
    co.mm.wait(s)
    co.sop(s.copy, xb.rearrange("p c n -> p (c n)"), xt.rearrange("p c n -> p (c n)"))
    e0 = co.sop(s.square, sq.rearrange("p c n -> p (c n)"),
                xt.rearrange("p c n -> p (c n)"))
    t.wait_ge(co.ev.h, e0)
    co.dv.wait(t)
    for dc in range(DC):
        t.matmul(ps_st[0:1, 0, :], C["ones128_b"], xb[:, dc, :],
                 start=(dc == 0), stop=(dc == DC - 1))
    for dc in range(DC):
        ins = t.matmul(ps_st[0:1, 1, :], C["ones128_b"], sq[:, dc, :],
                       start=(dc == 0), stop=(dc == DC - 1))
    m0 = co.mm.inc(ins)
    v.wait_ge(co.mm.h, m0)
    co.vop(v.tensor_scalar_mul, rows[:, 0, :n], ps_st[0:1, 0, :], 1.0 / DM)
    co.vop(v.tensor_scalar_mul, rows[:, 1, :n], ps_st[0:1, 1, :], 1.0 / DM)
    co.vop(v.tensor_mul, rows[:, 2, :n], rows[:, 0, :n], rows[:, 0, :n])
    d0 = co.vop(v.tensor_sub, rows[:, 1, :n], rows[:, 1, :n], rows[:, 2, :n])
    s.wait_ge(co.dv.h, d0)
    e1 = co.sop(s.activation, rows[:, 1, :n], rows[:, 1, :n], AF.Sqrt,
                bias=C["eps1"][0:1, 0:1])
    v.wait_ge(co.ev.h, e1)
    co.vop(v.reciprocal, rows[:, 1, :n], rows[:, 1, :n])
    d1 = co.vop(v.scalar_tensor_tensor, rows[:, 2, :n], rows[:, 0, :n], -1.0,
                rows[:, 1, :n], ALU.mult, ALU.mult)
    t.wait_ge(co.dv.h, d1)
    t.matmul(ps_bc[:, 0, :], C["onesrow_f"], rows[:, 1, :n], start=True, stop=True)
    m1 = co.mm.inc(t.matmul(ps_bc[:, 1, :], C["onesrow_f"], rows[:, 2, :n],
                            start=True, stop=True))
    v.wait_ge(co.mm.h, m1)
    for dc in range(DC):
        co.vop(v.tensor_mul, xn_out[:, dc, :], xt[:, dc, :], ps_bc[:, 0, :])
    for dc in range(DC):
        d = co.vop(v.tensor_tensor, xn_out[:, dc, :], xn_out[:, dc, :],
                   ps_bc[:, 1, :], ALU.add)
    return d


def emit_ln_pass(co, C, Xin, XN_out_hbm, xn_sbuf, ps6, bufs):
    s, v, y = co.s, co.v, co.y
    xt, xnt, xb, sq, rows = bufs
    ps_st = ps6[0:1, 0:2, :]
    ps_bc = ps6[:, 2:4, :]
    sttok = None
    for si in range(NSEQ):
        for tt in range(NTT):
            dst = xnt if xn_sbuf is None else \
                xn_sbuf[:, si, :, tt * TT:(tt + 1) * TT]
            lt = co.ld.put(xt[:, :, :], xap(Xin, si, tt))
            co.ld.wait(s, lt)
            co.ld.wait(v, lt)
            if sttok is not None and xn_sbuf is None:
                co.ld.wait(v, sttok)
            dtok = emit_ln_tile(co, C, xt, dst, ps_st, ps_bc, xb, sq, rows, TT)
            if XN_out_hbm is not None:
                y.wait_ge(co.dv.h, dtok)
                srcb = dst if xn_sbuf is not None else xnt
                sttok = co.ld.put(xap(XN_out_hbm, si, tt), srcb[:, :, :])
            co.ev.wait(y)
            co.dv.wait(y)
    return sttok


def emit_ffn(co, C, Xin, Xout, XN, w):
    nc = co.nc
    t, s, v, y = co.t, co.s, co.v, co.y
    H, HC = w["H"], w["H"] // 128
    kind = w["kind"]
    omul = w.get("out_mul", 1.0) * w["scale"]

    with (
        nc.sbuf_tensor(co.u("w1b"), [128, DC, H], BF16) as w1b,
        nc.sbuf_tensor(co.u("wub"), [128, DC, H if kind == "silu_mul" else 1], BF16) as wub,
        nc.sbuf_tensor(co.u("w2b"), [128, HC, DM], BF16) as w2b,
        nc.sbuf_tensor(co.u("b1s"), [128, HC], F32) as b1s,
        nc.sbuf_tensor(co.u("b2s"), [128, DC], F32) as b2s,
        nc.sbuf_tensor(co.u("xnt"), [128, DC, TT], BF16) as xnt,
        nc.sbuf_tensor(co.u("hbt"), [128, HC, TT], BF16) as hbt,
        nc.sbuf_tensor(co.u("xts"), [128, DC, TT], F32) as xts,
        nc.sbuf_tensor(co.u("gtmp"), [128, TT], BF16) as gtmp,
        nc.sbuf_tensor(co.u("ytmp"), [128, TT], F32) as ytmp,
        nc.sbuf_tensor(co.u("rows"), [1, 3, TT], F32) as rows,
        nc.psum_tensor(co.u("ps6"), [128, 6, TT], F32) as ps6,
    ):
        assert HC >= 2 * DC
        wtoks = [co.wq.put(w1b[:, kc, :], w["w1"][kc]) for kc in range(DC)]
        if kind == "silu_mul":
            wtoks += [co.wq.put(wub[:, kc, :], w["wu"][kc]) for kc in range(DC)]
        wtoks += [co.wq.put(w2b[:, hc, :], w["w2"][hc]) for hc in range(HC)]
        btoks = []
        if w.get("b1") is not None:
            btoks.append(co.ld.put(b1s[:, :], w["b1"]))
        if w.get("b2") is not None:
            btoks.append(co.ld.put(b2s[:, :], w["b2"]))

        ln_sttok = emit_ln_pass(co, C, Xin, XN, None, ps6,
                                (xts, xnt, hbt[:, 0:DC, :], hbt[:, DC:2 * DC, :], rows))

        for tok in wtoks:
            co.wq.wait(t, tok)
        for tok in btoks:
            co.ld.wait(s, tok)
            co.ld.wait(v, tok)
        co.dv.wait(t)

        psh = ps6[:, 0:4, :]
        psy = ps6[:, 4:6, :]
        prev_store = None
        prev_ytile_mm = 0
        prev_h_mark = 0
        ev_h_marks = {}
        dv_h_marks = {}
        dv_y_marks = {}
        for si in range(NSEQ):
            for tt in range(NTT):
                y.wait_ge(co.mm.h, prev_h_mark)
                if si == 0 and tt == 0 and ln_sttok is not None:
                    co.ld.wait(y, ln_sttok)
                xtok = co.ld.put(xnt[:, :, :], xap(XN, si, tt))
                co.ld.wait(t, xtok)

                for hc in range(HC):
                    if kind == "gelu":
                        pb = hc % 4
                        if ev_h_marks.get(pb):
                            t.wait_ge(co.ev.h, ev_h_marks[pb])
                        for kc in range(DC):
                            ins = t.matmul(psh[:, pb, :],
                                           w1b[:, kc, hc * 128:(hc + 1) * 128],
                                           xnt[:, kc, :],
                                           start=(kc == 0), stop=(kc == DC - 1))
                        mmk = co.mm.inc(ins)
                        s.wait_ge(co.mm.h, max(mmk, prev_ytile_mm))
                        ev_h_marks[pb] = co.sop(
                            s.activation, hbt[:, hc, :], psh[:, pb, :], AF.Gelu,
                            bias=b1s[:, hc:hc + 1])
                    else:
                        pb = (hc % 2) * 2
                        if dv_h_marks.get(pb):
                            t.wait_ge(co.dv.h, dv_h_marks[pb])
                        for kc in range(DC):
                            t.matmul(psh[:, pb, :],
                                     w1b[:, kc, hc * 128:(hc + 1) * 128],
                                     xnt[:, kc, :],
                                     start=(kc == 0), stop=(kc == DC - 1))
                        for kc in range(DC):
                            ins = t.matmul(psh[:, pb + 1, :],
                                           wub[:, kc, hc * 128:(hc + 1) * 128],
                                           xnt[:, kc, :],
                                           start=(kc == 0), stop=(kc == DC - 1))
                        mmk = co.mm.inc(ins)
                        s.wait_ge(co.mm.h, mmk)
                        if dv_h_marks.get("g"):
                            s.wait_ge(co.dv.h, dv_h_marks["g"])
                        e = co.sop(s.activation, gtmp[:, :], psh[:, pb, :], AF.Silu,
                                   scale=C["c127i"][:, 0:1])
                        v.wait_ge(co.ev.h, e)
                        if prev_ytile_mm:
                            v.wait_ge(co.mm.h, prev_ytile_mm)
                        dv_h_marks[pb] = dv_h_marks["g"] = co.vop(
                            v.tensor_mul, hbt[:, hc, :], gtmp[:, :], psh[:, pb + 1, :])

                prev_h_mark = co.mm.count
                if prev_store is not None:
                    co.ld.wait(y, prev_store)
                xrt = co.ld.put(xts[:, :, :], xap(Xin, si, tt))
                co.ld.wait(v, xrt)
                if kind == "gelu":
                    t.wait_ge(co.ev.h, co.ev.count)
                else:
                    t.wait_ge(co.dv.h, co.dv.count)
                for dc in range(DC):
                    pb = dc % 2
                    if dv_y_marks.get(pb):
                        t.wait_ge(co.dv.h, dv_y_marks[pb])
                    for hc in range(HC):
                        ins = t.matmul(psy[:, pb, :],
                                       w2b[:, hc, dc * 128:(dc + 1) * 128],
                                       hbt[:, hc, :],
                                       start=(hc == 0), stop=(hc == HC - 1))
                    mmk = co.mm.inc(ins)
                    v.wait_ge(co.mm.h, mmk)
                    if prev_store is not None and dc == 0:
                        co.ld.wait(v, prev_store)
                    if w.get("b2") is not None:
                        co.vop(v.tensor_scalar, ytmp[:, :], psy[:, pb, :],
                               b2s[:, dc:dc + 1], omul, ALU.add, ALU.mult)
                    else:
                        co.vop(v.tensor_scalar, ytmp[:, :], psy[:, pb, :], omul,
                               None, ALU.mult)
                    dv_y_marks[pb] = co.vop(v.tensor_tensor, xts[:, dc, :],
                                            xts[:, dc, :], ytmp[:, :], ALU.add)
                prev_ytile_mm = co.mm.count
                y.wait_ge(co.dv.h, co.dv.count)
                prev_store = co.ld.put(xap(Xout, si, tt), xts[:, :, :])
    co.join()


def emit_attn_l0(co, C, Xin, Xout, scratch, cfg):
    nc = co.nc
    t, s, v, g, y = co.t, co.s, co.v, co.g, co.y
    RFF = 32
    RS2 = (2.0 / (2 * RFF)) * math.exp(-0.5)

    with ExitStack() as _st:
        Vt = _st.enter_context(nc.sbuf_tensor(co.u("z_vt"), [128, NQT, DM], BF16))
        PHI = _st.enter_context(nc.sbuf_tensor(co.u("z_phi"), [64, SEQ], BF16))
        CXs = _st.enter_context(nc.sbuf_tensor(co.u("z_cx"), [128, DC, SEQ], BF16))
        Pb = _st.enter_context(nc.sbuf_tensor(co.u("z_pb"), [128, SEQ], BF16))
        PT = _st.enter_context(nc.sbuf_tensor(co.u("z_pt"), [128, NQT, 128], BF16))
        scf = _st.enter_context(nc.sbuf_tensor(co.u("z_sc"), [128, SEQ], F32))
        Tt = _st.enter_context(nc.sbuf_tensor(co.u("z_tt"), [128, SEQ], F32))
        rwb = _st.enter_context(nc.sbuf_tensor(co.u("z_rw"), [128, 8], F32))
        ivt = _st.enter_context(nc.sbuf_tensor(co.u("z_iv"), [128, 1], F32))
        ps8 = _st.enter_context(nc.psum_tensor(co.u("z_ps"), [128, 8, TT], F32))
        lt1 = _st.enter_context(nc.sbuf_tensor(co.u("z_lt"), [1, 8], F32))
        ltok = co.ld.put(lt1[0:1, 0:1], cfg["log_temp"])
        co.ld.wait(v, ltok)
        co.ld.wait(s, ltok)
        co.vop(v.tensor_scalar_mul, lt1[0:1, 1:2], lt1[0:1, 0:1], -1.0)
        co.vop(v.tensor_max, lt1[0:1, 1:2], lt1[0:1, 1:2], lt1[0:1, 0:1])
        d0 = co.dv.count
        s.wait_ge(co.dv.h, d0)
        co.sop(s.activation, lt1[0:1, 2:3], lt1[0:1, 1:2], AF.Exp,
               scale=C["cm1"][0:1, 0:1])
        co.sop(s.activation, lt1[0:1, 3:4], lt1[0:1, 2:3], AF.Ln,
               bias=C["c1"][0:1, 0:1])
        v.wait_ge(co.ev.h, co.ev.count)
        co.vop(v.tensor_single_scalar, lt1[0:1, 4:5], lt1[0:1, 0:1], 0.0, ALU.max)
        co.vop(v.tensor_add, lt1[0:1, 4:5], lt1[0:1, 4:5], lt1[0:1, 3:4])
        co.vop(v.tensor_scalar_add, lt1[0:1, 4:5], lt1[0:1, 4:5], 0.1)
        co.vop(v.reciprocal, lt1[0:1, 5:6], lt1[0:1, 4:5])
        co.vop(v.tensor_scalar_mul, lt1[0:1, 5:6], lt1[0:1, 5:6], RS2)
        d1 = co.dv.count
        t.wait_ge(co.dv.h, d1)
        mk = co.mm.inc(t.matmul(ps8[:, 7, 0:1], C["onesrow_f"], lt1[0:1, 5:6],
                                start=True, stop=True))
        v.wait_ge(co.mm.h, mk)
        co.vop(v.tensor_copy, ivt[:, :], ps8[:, 7, 0:1])
        for si in range(NSEQ):
            with ExitStack() as _s2:
                xts = _s2.enter_context(nc.sbuf_tensor(co.u("z_xt"), [128, DC, TT], F32))
                xnt = _s2.enter_context(nc.sbuf_tensor(co.u("z_xn"), [128, DC, TT], BF16))
                xb = _s2.enter_context(nc.sbuf_tensor(co.u("z_xb"), [128, DC, TT], BF16))
                sq = _s2.enter_context(nc.sbuf_tensor(co.u("z_sq"), [128, DC, TT], BF16))
                rows = _s2.enter_context(nc.sbuf_tensor(co.u("z_rs"), [1, 3, TT], F32))
                wv = _s2.enter_context(nc.sbuf_tensor(co.u("z_wv"), [128, DC, DM], BF16))
                wch = _s2.enter_context(nc.sbuf_tensor(co.u("z_wc"), [128, DC, 1], BF16))
                wsh = _s2.enter_context(nc.sbuf_tensor(co.u("z_ws"), [128, DC, 64], BF16))
                omT = _s2.enter_context(nc.sbuf_tensor(co.u("z_om"), [64, RFF], BF16))
                wcm = _s2.enter_context(nc.sbuf_tensor(co.u("z_wm"), [64, 5, 64], BF16))
                bsh = _s2.enter_context(nc.sbuf_tensor(co.u("z_bs"), [64, 1], F32))
                rfb = _s2.enter_context(nc.sbuf_tensor(co.u("z_rf"), [32, 1], F32))
                bvb = _s2.enter_context(nc.sbuf_tensor(co.u("z_bv"), [128, DM], F32))
                shl = _s2.enter_context(nc.sbuf_tensor(co.u("z_sh"), [64, TT], F32))
                ssq = _s2.enter_context(nc.sbuf_tensor(co.u("z_s2"), [64, TT], F32))
                shn = _s2.enter_context(nc.sbuf_tensor(co.u("z_sn"), [64, TT], BF16))
                prj = _s2.enter_context(nc.sbuf_tensor(co.u("z_pj"), [32, 2, TT], F32))
                phs = _s2.enter_context(nc.sbuf_tensor(co.u("z_ph"), [64, TT], BF16))
                pit = _s2.enter_context(nc.sbuf_tensor(co.u("z_pi"), [32, TT], I32))
                prow = _s2.enter_context(nc.sbuf_tensor(co.u("z_pr"), [1, 4, TT], F32))
                pcs = _s2.enter_context(nc.sbuf_tensor(co.u("z_pc"), [64, TT], F32))
                pacc = _s2.enter_context(nc.sbuf_tensor(co.u("z_pa"), [64, TT], F32))
                wt = [co.wq.put(wv[:, kc, :], cfg["wv"][kc]) for kc in range(DC)]
                wt += [co.wq.put(wch[:, kc, :], cfg["wch"][kc]) for kc in range(DC)]
                wt += [co.wq.put(wsh[:, kc, :], cfg["wsh"][kc]) for kc in range(DC)]
                wt.append(co.wq.put(omT[:, :], cfg["omT"]))
                wt.append(co.wq.put(wcm[:, :, :], cfg["wcm"].rearrange("m k n -> k m n")))
                bt = [co.ld.put(bsh[:, :], cfg["bsh"]), co.ld.put(rfb[:, :], cfg["rfb"]),
                      co.ld.put(bvb[0:1, :], cfg["bv_row"]),
                      co.ld.put(lt1[0:1, 6:7], cfg["bch"])]
                for tok in bt:
                    co.ld.wait(v, tok)
                    co.ld.wait(s, tok)
                co.ld.wait(t, bt[2])
                co.dv.wait(t)
                co.ev.wait(t)
                t.matmul(ps8[:, 6, :], C["onesrow_f"], bvb[0:1, 0:TT],
                         start=True, stop=True)
                mk = co.mm.inc(t.matmul(ps8[:, 7, :], C["onesrow_f"],
                                        bvb[0:1, TT:2 * TT], start=True, stop=True))
                v.wait_ge(co.mm.h, mk)
                co.vop(v.tensor_copy, bvb[:, 0:TT], ps8[:, 6, :])
                co.vop(v.tensor_copy, bvb[:, TT:2 * TT], ps8[:, 7, :])
                for tok in wt:
                    co.wq.wait(t, tok)
                for tt in range(NTT):
                    sl = slice(tt * TT, (tt + 1) * TT)
                    lt_ = co.ld.put(xts[:, :, :], xap(Xin, si, tt))
                    co.ld.wait(s, lt_)
                    co.ld.wait(v, lt_)
                    emit_ln_tile(co, C, xts, xnt, ps8[0:1, 0:2, :],
                                 ps8[:, 2:4, :], xb, sq, rows, TT)
                    co.dv.wait(t)
                    for tc in range(TT // 128):
                        tcg = tt * (TT // 128) + tc
                        co.dv.wait(t)
                        for kc in range(DC):
                            for vc in range(2):
                                ins = t.matmul(
                                    ps8[:, 4 + vc, :],
                                    xnt[:, kc, tc * 128:(tc + 1) * 128],
                                    wv[:, kc, vc * TT:(vc + 1) * TT],
                                    start=(kc == 0), stop=(kc == DC - 1))
                        mk = co.mm.inc(ins)
                        v.wait_ge(co.mm.h, mk)
                        for vc in range(2):
                            co.vop(v.tensor_tensor, Vt[:, tcg, vc * TT:(vc + 1) * TT],
                                   ps8[:, 4 + vc, :], bvb[:, vc * TT:(vc + 1) * TT],
                                   ALU.add)
                    co.dv.wait(t)
                    for kc in range(DC):
                        ins = t.matmul(ps8[0:1, 6, :], wch[:, kc, :], xnt[:, kc, :],
                                       start=(kc == 0), stop=(kc == DC - 1))
                    mkq = co.mm.inc(ins)
                    s.wait_ge(co.mm.h, mkq)
                    co.sop(s.activation, prow[:, 0, :], ps8[0:1, 6, :], AF.Tanh,
                           bias=lt1[0:1, 6:7])
                    for kc in range(DC):
                        ins = t.matmul(ps8[0:64, 7, :], wsh[:, kc, :], xnt[:, kc, :],
                                       start=(kc == 0), stop=(kc == DC - 1))
                    mksh = co.mm.inc(ins)
                    v.wait_ge(co.mm.h, mksh)
                    co.vop(v.tensor_scalar_add, shl[:, :], ps8[0:64, 7, :],
                           bsh[:, 0:1])
                    d_ = co.dv.count
                    s.wait_ge(co.dv.h, d_)
                    co.sop(s.square, ssq[:, :], shl[:, :])
                    co.ev.wait(t)
                    ins = t.matmul(ps8[0:1, 6, :], C["ones64_f"],
                                   ssq[:, :], start=True, stop=True)
                    mkn = co.mm.inc(ins)
                    s.wait_ge(co.mm.h, mkn)
                    co.sop(s.activation, prow[:, 1, :], ps8[0:1, 6, :], AF.Sqrt)
                    v.wait_ge(co.ev.h, co.ev.count)
                    co.vop(v.tensor_single_scalar, prow[:, 1, :], prow[:, 1, :],
                           1e-12, ALU.max)
                    co.vop(v.reciprocal, prow[:, 1, :], prow[:, 1, :])
                    d_ = co.dv.count
                    t.wait_ge(co.dv.h, d_)
                    ins = t.matmul(ps8[0:64, 7, :], C["onesrow_f"][0:1, 0:64],
                                   prow[:, 1, :], start=True, stop=True)
                    mkb = co.mm.inc(ins)
                    v.wait_ge(co.mm.h, mkb)
                    co.vop(v.tensor_mul, shn[:, :], shl[:, :], ps8[0:64, 7, :])
                    d_ = co.dv.count
                    t.wait_ge(co.dv.h, d_)
                    ins = t.matmul(ps8[0:32, 6, :], omT[:, :], shn[:, :],
                                   start=True, stop=True)
                    mkp = co.mm.inc(ins)
                    v.wait_ge(co.mm.h, mkp)
                    co.vop(v.tensor_scalar_add, prj[:, 0, :], ps8[0:32, 6, :],
                           rfb[:, 0:1])
                    for part, shift in ((1, 16.5), (0, 16.75)):
                        co.vop(v.tensor_scalar, prj[:, 1, :], prj[:, 0, :],
                               1.0 / (2 * math.pi), shift, ALU.mult, ALU.add)
                        co.vop(v.tensor_copy, pit[:, :], prj[:, 1, :])
                        co.vop(v.scalar_tensor_tensor, prj[:, 1, :], pit[:, :],
                               0.5, prj[:, 1, :], ALU.add, ALU.subtract)
                        d2 = co.dv.count
                        s.wait_ge(co.dv.h, d2)
                        lo = 32 * part
                        co.sop(s.activation, phs[32 - lo:64 - lo, :],
                               prj[:, 1, :], AF.Sin,
                               scale=C["cm2pi"][32 - lo:64 - lo, 0:1])
                        co.ev.wait(v)
                    co.ev.wait(v)
                    co.vop(v.tensor_copy, prow[:, 2, :], prow[:, 0, :])
                    for m in range(5):
                        co.ev.wait(t)
                        co.dv.wait(t)
                        ins = t.matmul(ps8[0:64, 6, :], wcm[:, m, :], phs[:, :],
                                       start=True, stop=True)
                        mkm = co.mm.inc(ins)
                        if m == 0:
                            v.wait_ge(co.mm.h, mkm)
                            co.vop(v.tensor_scalar_mul, pacc[:, :],
                                   ps8[0:64, 6, :], cfg["mags"][0])
                        else:
                            co.vop(v.tensor_scalar_mul, prow[:, 3, :],
                                   prow[:, 2, :], cfg["mags"][m])
                            d4 = co.dv.count
                            t.wait_ge(co.dv.h, d4)
                            ins2 = t.matmul(ps8[0:64, 7, :],
                                            C["onesrow_f"][0:1, 0:64],
                                            prow[:, 3, :], start=True, stop=True)
                            mkb2 = co.mm.inc(ins2)
                            v.wait_ge(co.mm.h, max(mkm, mkb2))
                            co.vop(v.tensor_copy, pcs[:, :], ps8[0:64, 7, :])
                            co.vop(v.tensor_mul, pcs[:, :], pcs[:, :],
                                   ps8[0:64, 6, :])
                            co.vop(v.tensor_add, pacc[:, :], pacc[:, :], pcs[:, :])
                            if m < 4:
                                co.vop(v.tensor_mul, prow[:, 2, :], prow[:, 2, :],
                                       prow[:, 0, :])
                    co.vop(v.tensor_copy, PHI[:, sl], pacc[:, :])
                    co.ev.wait(y)
                    co.dv.wait(y)
            co.join()
            scf_mark = 0
            for qt in range(NQT):
                nkc = qt + 1
                W = nkc * 128
                co.dv.wait(y)
                ttok = co.ld.put(Tt[:, 0:W], cfg["Tq"][qt, :, 0:W])
                if scf_mark:
                    t.wait_ge(co.dv.h, scf_mark)
                for kg in range((W + TT - 1) // TT):
                    w0 = kg * TT
                    w1 = min(W, w0 + TT)
                    ins = t.matmul(ps8[:, kg, 0:w1 - w0],
                                   PHI[:, qt * 128:(qt + 1) * 128],
                                   PHI[:, w0:w1], start=True, stop=True)
                mks = co.mm.inc(ins)
                v.wait_ge(co.mm.h, mks)
                co.ld.wait(v, ttok)
                for kg in range((W + TT - 1) // TT):
                    w0 = kg * TT
                    w1 = min(W, w0 + TT)
                    scf_mark = co.vop(v.tensor_mul, scf[:, w0:w1],
                                      ps8[:, kg, 0:w1 - w0], Tt[:, w0:w1])
                co.vop(v.reduce_max, rwb[:, 0:1], scf[:, 0:W], AX.X, negate=True)
                co.vop(v.tensor_mul, rwb[:, 1:2], rwb[:, 0:1], ivt[:, :])
                d_ = co.dv.count
                s.wait_ge(co.dv.h, d_)
                co.sop(s.activation, Pb[:, 0:W], scf[:, 0:W], AF.Exp,
                       bias=rwb[:, 1:2], scale=ivt[:, 0:1],
                       accum_out=rwb[:, 2:3])
                co.ev.wait(v)
                dg = Pb[:, qt * 128:(qt + 1) * 128]
                co.vop(v.reduce_sum, rwb[:, 3:4], dg, AX.X)
                co.vop(v.tensor_tensor, dg, dg, C["tril_b"], ALU.mult)
                co.vop(v.reduce_sum, rwb[:, 4:5], dg, AX.X)
                co.vop(v.tensor_sub, rwb[:, 2:3], rwb[:, 2:3], rwb[:, 3:4])
                co.vop(v.tensor_add, rwb[:, 2:3], rwb[:, 2:3], rwb[:, 4:5])
                co.vop(v.reciprocal, rwb[:, 6:7], rwb[:, 2:3])
                nrm_mark = co.vop(v.tensor_scalar_mul, Pb[:, 0:W], Pb[:, 0:W],
                                  rwb[:, 6:7])
                t.wait_ge(co.dv.h, nrm_mark)
                cp_marks = {}
                for kc in range(nkc):
                    bk = 4 if kc % 2 == 0 else 6
                    view = ps8[:, bk, :].bitcast(BF16)[:, 0:128]
                    if kc >= 2:
                        t.wait_ge(co.dv.h, cp_marks[kc - 2])
                    ins = t.transpose(view,
                                      Pb[:, kc * 128:(kc + 1) * 128], C["ident_b"])
                    mkt = co.mm.inc(ins)
                    v.wait_ge(co.mm.h, mkt)
                    cp_marks[kc] = co.vop(v.tensor_copy, PT[:, kc, :], view)
                t.wait_ge(co.dv.h, co.dv.count)
                for dc in range(DC):
                    dslot = dc % 4
                    co.ev.wait(t)
                    for kc in range(nkc):
                        ins = t.matmul(ps8[:, 5, dslot * 128:(dslot + 1) * 128],
                                       Vt[:, kc, dc * 128:(dc + 1) * 128],
                                       PT[:, kc, :], start=(kc == 0),
                                       stop=(kc == nkc - 1))
                    mka = co.mm.inc(ins)
                    s.wait_ge(co.mm.h, mka)
                    co.sop(s.copy, CXs[:, dc, qt * 128:(qt + 1) * 128],
                           ps8[:, 5, dslot * 128:(dslot + 1) * 128])
            co.join()
            with ExitStack() as _s3:
                wo = _s3.enter_context(nc.sbuf_tensor(co.u("z_wo"), [128, DC, DM], BF16))
                bo = _s3.enter_context(nc.sbuf_tensor(co.u("z_bo"), [128, 8], F32))
                xrt = _s3.enter_context(nc.sbuf_tensor(co.u("z_xr"), [128, DC, TT], F32))
                wt = [co.wq.put(wo[:, kc, :], cfg["wo"][kc]) for kc in range(DC)]
                bt = co.ld.put(bo[:, :], cfg["bo"])
                co.ld.wait(v, bt)
                pst = None
                for tt in range(NTT):
                    if pst is not None:
                        co.ld.wait(y, pst)
                    xt_ = co.ld.put(xrt[:, :, :], xap(Xin, si, tt))
                    if tt == 0:
                        for tok in wt:
                            co.wq.wait(t, tok)
                    co.ld.wait(v, xt_)
                    for oc in range(DC):
                        pb = oc % 2
                        co.dv.wait(t)
                        for kc in range(DC):
                            ins = t.matmul(ps8[:, 6 + pb, :],
                                           wo[:, kc, oc * 128:(oc + 1) * 128],
                                           CXs[:, kc, tt * TT:(tt + 1) * TT],
                                           start=(kc == 0), stop=(kc == DC - 1))
                        mk = co.mm.inc(ins)
                        v.wait_ge(co.mm.h, mk)
                        co.vop(v.scalar_tensor_tensor, xrt[:, oc, :],
                               ps8[:, 6 + pb, :], bo[:, oc:oc + 1],
                               xrt[:, oc, :], ALU.add, ALU.add)
                    co.dv.wait(y)
                    pst = co.ld.put(xap(Xout, si, tt), xrt[:, :, :])
            co.join()


def emit_moe(co, C, Xin, Xout, scratch, cfg):
    nc = co.nc
    t, s, v, g, y = co.t, co.s, co.v, co.g, co.y
    kind = cfg["kind"]
    XN, AACC, SSP = scratch["XN"], scratch["AACC"], scratch["SSP"]
    NE = 8

    with ExitStack() as _st:
        MR = _st.enter_context(nc.sbuf_tensor(co.u("mr"), [8, NSEQ * SEQ], F32))
        ps8 = _st.enter_context(nc.psum_tensor(co.u("m_ps"), [128, 8, TT], F32))
        with ExitStack() as _s2:
            xts = _s2.enter_context(nc.sbuf_tensor(co.u("m_xt"), [128, DC, TT], F32))
            xnt = _s2.enter_context(nc.sbuf_tensor(co.u("m_xn"), [128, DC, TT], BF16))
            xb = _s2.enter_context(nc.sbuf_tensor(co.u("m_xb"), [128, DC, TT], BF16))
            sq = _s2.enter_context(nc.sbuf_tensor(co.u("m_sq"), [128, DC, TT], BF16))
            rows = _s2.enter_context(nc.sbuf_tensor(co.u("m_rw"), [1, 3, TT], F32))
            rWs = _s2.enter_context(nc.sbuf_tensor(co.u("m_rW"), [128, DC, 8], BF16))
            gbB = _s2.enter_context(nc.sbuf_tensor(co.u("m_gb"), [128, 8], F32))
            mt = _s2.enter_context(nc.sbuf_tensor(co.u("m_mt"), [128, 12, 8], F32))
            mrow = _s2.enter_context(nc.sbuf_tensor(co.u("m_mr"), [128, 4], F32))
            rt = [co.wq.put(rWs[:, kc, :], cfg["router"][kc]) for kc in range(DC)]
            if kind == "l3":
                gb = co.ld.put(gbB[0:1, :], cfg["gate_b_row"])
                co.ld.wait(t, gb)
                co.dv.wait(t)
                mk = co.mm.inc(t.matmul(ps8[:, 7, 0:8], C["onesrow_f"],
                                        gbB[0:1, :], start=True, stop=True))
                v.wait_ge(co.mm.h, mk)
                co.vop(v.tensor_copy, gbB[:, :], ps8[:, 7, 0:8])
            for tok in rt:
                co.wq.wait(t, tok)
            sttok = None
            for si in range(NSEQ):
                for tt in range(NTT):
                    lt = co.ld.put(xts[:, :, :], xap(Xin, si, tt))
                    co.ld.wait(s, lt)
                    co.ld.wait(v, lt)
                    if sttok is not None:
                        co.ld.wait(v, sttok)
                    dtok = emit_ln_tile(co, C, xts, xnt, ps8[0:1, 0:2, :],
                                        ps8[:, 2:4, :], xb, sq, rows, TT)
                    y.wait_ge(co.dv.h, dtok)
                    sttok = co.ld.put(xap(XN, si, tt), xnt[:, :, :])
                    for tc in range(TT // 128):
                        gtok = (si * SEQ + tt * TT + tc * 128)
                        co.dv.wait(t)
                        for kc in range(DC):
                            ins = t.matmul(ps8[:, 5, 0:8],
                                           xnt[:, kc, tc * 128:(tc + 1) * 128],
                                           rWs[:, kc, :],
                                           start=(kc == 0), stop=(kc == DC - 1))
                        mk = co.mm.inc(ins)
                        v.wait_ge(co.mm.h, mk)
                        lg = mt[:, 0, :]
                        if kind == "l3":
                            co.vop(v.tensor_tensor, lg, ps8[:, 5, 0:8], gbB[:, :],
                                   ALU.add)
                        else:
                            co.vop(v.tensor_copy, lg, ps8[:, 5, 0:8])
                        co.vop(v.max, mt[:, 2, :], lg)
                        co.vop(v.memset, mt[:, 3, :], -1e30)
                        co.vop(v.tensor_copy, mt[:, 3, 0:1], mt[:, 2, 0:1])
                        co.vop(v.match_replace, mt[:, 4, :], mt[:, 3, :], lg, -1e30)
                        if kind == "l2":
                            co.vop(v.tensor_tensor, mt[:, 6, :], lg, mt[:, 4, :],
                                   ALU.not_equal)
                            co.vop(v.tensor_scalar_mul, mt[:, 1, :], mt[:, 6, :], 0.5)
                        else:
                            co.vop(v.tensor_copy, mt[:, 3, 1:2], mt[:, 2, 1:2])
                            co.vop(v.match_replace, mt[:, 5, :], mt[:, 3, :], lg,
                                   -1e30)
                            co.vop(v.tensor_tensor, mt[:, 6, :], lg, mt[:, 4, :],
                                   ALU.not_equal)
                            co.vop(v.tensor_tensor, mt[:, 7, :], lg, mt[:, 5, :],
                                   ALU.not_equal)
                            co.vop(v.tensor_sub, mt[:, 7, :], mt[:, 7, :],
                                   mt[:, 6, :])
                            co.vop(v.tensor_sub, mrow[:, 0:1], mt[:, 2, 0:1],
                                   mt[:, 2, 1:2])
                            d0 = co.dv.count
                            s.wait_ge(co.dv.h, d0)
                            co.sop(s.activation, mrow[:, 0:1], mrow[:, 0:1],
                                   AF.Sigmoid)
                            v.wait_ge(co.ev.h, co.ev.count)
                            co.vop(v.tensor_scalar, mrow[:, 2:3], mrow[:, 0:1],
                                   -1.0, 1.0, ALU.mult, ALU.add)
                            co.vop(v.tensor_scalar, mt[:, 6, :], mt[:, 6, :],
                                   mrow[:, 0:1], 0.5, ALU.mult, ALU.mult)
                            co.vop(v.tensor_scalar, mt[:, 7, :], mt[:, 7, :],
                                   mrow[:, 2:3], 0.5, ALU.mult, ALU.mult)
                            co.vop(v.tensor_add, mt[:, 1, :], mt[:, 6, :],
                                   mt[:, 7, :])
                        d1 = co.dv.count
                        t.wait_ge(co.dv.h, d1)
                        ins = t.transpose(ps8[0:8, 6, 0:128], mt[:, 1, :],
                                          C["ident_f"])
                        mk2 = co.mm.inc(ins)
                        v.wait_ge(co.mm.h, mk2)
                        co.vop(v.tensor_copy, MR[:, gtok:gtok + 128],
                               ps8[0:8, 6, 0:128])
        co.join()
        a_store = {}
        for e in range(NE):
            if kind == "l2":
                _emit_moe_l2_expert(co, C, Xin, Xout, XN, AACC, MR, ps8, cfg,
                                    e, NE, a_store)
            else:
                _emit_moe_l3_expert(co, C, Xin, Xout, XN, AACC, SSP, MR, ps8,
                                    cfg, e, NE, a_store)
            co.join()


def _emit_moe_l2_expert(co, C, Xin, Xout, XN, AACC, MR, ps8, cfg, e, NE, a_store):
    nc = co.nc
    t, s, v, y = co.t, co.s, co.v, co.y
    HC = 22
    with ExitStack() as _st:
        gW = _st.enter_context(nc.sbuf_tensor(co.u("e_gw"), [128, DC, 2816], BF16))
        uW = _st.enter_context(nc.sbuf_tensor(co.u("e_uw"), [128, DC, 2816], BF16))
        dW = _st.enter_context(nc.sbuf_tensor(co.u("e_dw"), [128, HC, DM], BF16))
        xnt = _st.enter_context(nc.sbuf_tensor(co.u("e_xn"), [128, DC, TT], BF16))
        hbt = _st.enter_context(nc.sbuf_tensor(co.u("e_hb"), [128, HC, TT], BF16))
        at = _st.enter_context(nc.sbuf_tensor(co.u("e_at"), [128, DC, TT], F32))
        gtmp = _st.enter_context(nc.sbuf_tensor(co.u("e_gt"), [128, TT], BF16))
        mbs = _st.enter_context(nc.sbuf_tensor(co.u("e_mb"), [128, TT], F32))
        ytm = _st.enter_context(nc.sbuf_tensor(co.u("e_yt"), [128, TT], F32))
        wt = [co.wq.put(gW[:, kc, :], cfg["gateW"][e, kc]) for kc in range(DC)]
        wt += [co.wq.put(uW[:, kc, :], cfg["upW"][e, kc]) for kc in range(DC)]
        wt += [co.wq.put(dW[:, hc, :], cfg["downW"][e, hc]) for hc in range(HC)]
        for tok in wt:
            co.wq.wait(t, tok)
        psh = ps8[:, 0:4, :]
        psy = ps8[:, 4:6, :]
        psm = ps8[:, 6, :]
        evm, dvm, dvy = {}, {}, {}
        mb_mark = 0
        for si in range(NSEQ):
            for tt in range(NTT):
                co.mm.wait(y)
                xtok = co.ld.put(xnt[:, :, :], xap(XN, si, tt))
                co.ld.wait(t, xtok)
                for hc in range(HC):
                    pb = (hc % 2) * 2
                    if evm.get(pb):
                        t.wait_ge(co.ev.h, evm[pb])
                    if dvm.get(pb):
                        t.wait_ge(co.dv.h, dvm[pb])
                    for kc in range(DC):
                        t.matmul(psh[:, pb, :], gW[:, kc, hc * 128:(hc + 1) * 128],
                                 xnt[:, kc, :], start=(kc == 0), stop=(kc == DC - 1))
                    for kc in range(DC):
                        ins = t.matmul(psh[:, pb + 1, :],
                                       uW[:, kc, hc * 128:(hc + 1) * 128],
                                       xnt[:, kc, :], start=(kc == 0),
                                       stop=(kc == DC - 1))
                    mmk = co.mm.inc(ins)
                    s.wait_ge(co.mm.h, mmk)
                    if dvm.get("g"):
                        s.wait_ge(co.dv.h, dvm["g"])
                    evm[pb] = co.sop(s.activation, gtmp[:, :], psh[:, pb, :], AF.Silu)
                    v.wait_ge(co.ev.h, evm[pb])
                    dvm[pb] = dvm["g"] = co.vop(v.tensor_mul, hbt[:, hc, :],
                                                gtmp[:, :], psh[:, pb + 1, :])
                gtok = si * SEQ + tt * TT
                if mb_mark:
                    t.wait_ge(co.dv.h, mb_mark)
                ins = t.matmul(psm, C["sel8"][:, e * 128:(e + 1) * 128],
                               MR[:, gtok:gtok + TT], start=True, stop=True)
                mkb = co.mm.inc(ins)
                prev = a_store.get((si, tt))
                if prev is not None:
                    co.ld.wait(y, prev)
                if a_store.get("last") is not None:
                    co.ld.wait(y, a_store["last"])
                at_tok = co.ld.put(at[:, :, :],
                                   xap(Xin if e == 0 else AACC, si, tt))
                co.ld.wait(v, at_tok)
                v.wait_ge(co.mm.h, mkb)
                mb_mark = co.vop(v.tensor_copy, mbs[:, :], psm)
                t.wait_ge(co.dv.h, dvm["g"])
                for dc in range(DC):
                    pb = dc % 2
                    if dvy.get(pb):
                        t.wait_ge(co.dv.h, dvy[pb])
                    for hc in range(HC):
                        ins = t.matmul(psy[:, pb, :],
                                       dW[:, hc, dc * 128:(dc + 1) * 128],
                                       hbt[:, hc, :], start=(hc == 0),
                                       stop=(hc == HC - 1))
                    mmk = co.mm.inc(ins)
                    v.wait_ge(co.mm.h, mmk)
                    co.vop(v.tensor_mul, ytm[:, :], psy[:, pb, :], mbs[:, :])
                    dvy[pb] = co.vop(v.tensor_tensor, at[:, dc, :], at[:, dc, :],
                                     ytm[:, :], ALU.add)
                co.dv.wait(y)
                a_store[(si, tt)] = a_store["last"] = co.ld.put(
                    xap(Xout if e == NE - 1 else AACC, si, tt), at[:, :, :])


def _emit_moe_l3_expert(co, C, Xin, Xout, XN, AACC, SSP, MR, ps8, cfg, e, NE,
                        a_store):
    nc = co.nc
    t, s, v, y = co.t, co.s, co.v, co.y
    FC = 16
    with ExitStack() as _st:
        w1 = _st.enter_context(nc.sbuf_tensor(co.u("e_w1"), [128, DC, 2048], BF16))
        w2 = _st.enter_context(nc.sbuf_tensor(co.u("e_w2"), [128, DC, 2048], BF16))
        wg = _st.enter_context(nc.sbuf_tensor(co.u("e_wg"), [128, FC, 2048], BF16))
        bb = _st.enter_context(nc.sbuf_tensor(co.u("e_bb"), [128, 3, FC], F32))
        xnt = _st.enter_context(nc.sbuf_tensor(co.u("e_xn"), [128, DC, TT], BF16))
        x1s = _st.enter_context(nc.sbuf_tensor(co.u("e_x1"), [128, TT], F32))
        zb = _st.enter_context(nc.sbuf_tensor(co.u("e_zb"), [128, FC, TT], BF16))
        sb = _st.enter_context(nc.sbuf_tensor(co.u("e_sb"), [128, FC, TT], BF16))
        wt = [co.wq.put(w1[:, kc, :], cfg["w1"][e, kc]) for kc in range(DC)]
        wt += [co.wq.put(w2[:, kc, :], cfg["w2"][e, kc]) for kc in range(DC)]
        wt += [co.wq.put(wg[:, fc, :], cfg["wg"][e, fc]) for fc in range(FC)]
        bt = [co.ld.put(bb[:, 0, :], cfg["b1"][e]),
              co.ld.put(bb[:, 1, :], cfg["b2"][e]),
              co.ld.put(bb[:, 2, :], cfg["bg"][e])]
        for tok in wt:
            co.wq.wait(t, tok)
        for tok in bt:
            co.ld.wait(v, tok)
            co.ld.wait(s, tok)
        ps4 = ps8[:, 0:4, :]
        pss = ps8[:, 4:6, :]
        st_tok = None
        dvm, evs = {}, {}
        for si in range(NSEQ):
            for tt in range(NTT):
                co.mm.wait(y)
                xtok = co.ld.put(xnt[:, :, :], xap(XN, si, tt))
                co.ld.wait(t, xtok)
                for fc in range(FC):
                    pb = (fc % 2) * 2
                    if dvm.get(pb):
                        t.wait_ge(co.dv.h, dvm[pb])
                    for kc in range(DC):
                        t.matmul(ps4[:, pb, :], w1[:, kc, fc * 128:(fc + 1) * 128],
                                 xnt[:, kc, :], start=(kc == 0), stop=(kc == DC - 1))
                    for kc in range(DC):
                        ins = t.matmul(ps4[:, pb + 1, :],
                                       w2[:, kc, fc * 128:(fc + 1) * 128],
                                       xnt[:, kc, :], start=(kc == 0),
                                       stop=(kc == DC - 1))
                    mmk = co.mm.inc(ins)
                    v.wait_ge(co.mm.h, mmk)
                    co.vop(v.tensor_scalar_add, x1s[:, :], ps4[:, pb, :],
                           bb[:, 0, fc:fc + 1])
                    dvm[pb] = co.vop(v.scalar_tensor_tensor, zb[:, fc, :],
                                     ps4[:, pb + 1, :], bb[:, 1, fc:fc + 1],
                                     x1s[:, :], ALU.add, ALU.mult)
                if st_tok is not None:
                    co.ld.wait(s, st_tok)
                for sc in range(FC):
                    pb = sc % 2
                    if evs.get(pb):
                        t.wait_ge(co.ev.h, evs[pb])
                    if sc == 0:
                        t.wait_ge(co.dv.h, max(dvm[0], dvm[2]))
                    for fc in range(FC):
                        ins = t.matmul(pss[:, pb, :],
                                       wg[:, fc, sc * 128:(sc + 1) * 128],
                                       zb[:, fc, :], start=(fc == 0),
                                       stop=(fc == FC - 1))
                    mmk = co.mm.inc(ins)
                    s.wait_ge(co.mm.h, mmk)
                    evs[pb] = co.sop(s.activation, sb[:, sc, :], pss[:, pb, :],
                                     AF.Silu, bias=bb[:, 2, sc:sc + 1])
                y.wait_ge(co.ev.h, co.ev.count)
                st_tok = co.ld.put(
                    SSP[si, :, :, tt * TT:(tt + 1) * TT].rearrange("c p n -> p c n"),
                    sb[:, :, :])
        co.join()
    with ExitStack() as _st:
        w3 = _st.enter_context(nc.sbuf_tensor(co.u("e_w3"), [128, FC, DM], BF16))
        b3 = _st.enter_context(nc.sbuf_tensor(co.u("e_b3"), [128, 8], F32))
        sbt = _st.enter_context(nc.sbuf_tensor(co.u("e_st"), [128, FC, TT], BF16))
        at = _st.enter_context(nc.sbuf_tensor(co.u("e_at"), [128, DC, TT], F32))
        mbs = _st.enter_context(nc.sbuf_tensor(co.u("e_mb"), [128, TT], F32))
        ytm = _st.enter_context(nc.sbuf_tensor(co.u("e_yt"), [128, TT], F32))
        wt = [co.wq.put(w3[:, fc, :], cfg["w3"][e, fc]) for fc in range(FC)]
        bt = co.ld.put(b3[:, :], cfg["b3"][e])
        for tok in wt:
            co.wq.wait(t, tok)
        co.ld.wait(v, bt)
        psy = ps8[:, 0:2, :]
        psm = ps8[:, 6, :]
        dvy = {}
        mb_mark = 0
        for si in range(NSEQ):
            for tt in range(NTT):
                co.mm.wait(y)
                stok = co.ld.put(
                    sbt[:, :, :],
                    SSP[si, :, :, tt * TT:(tt + 1) * TT].rearrange("c p n -> p c n"))
                co.ld.wait(t, stok)
                gtok = si * SEQ + tt * TT
                if mb_mark:
                    t.wait_ge(co.dv.h, mb_mark)
                ins = t.matmul(psm, C["sel8"][:, e * 128:(e + 1) * 128],
                               MR[:, gtok:gtok + TT], start=True, stop=True)
                mkb = co.mm.inc(ins)
                prev = a_store.get((si, tt))
                if prev is not None:
                    co.ld.wait(y, prev)
                if a_store.get("last") is not None:
                    co.ld.wait(y, a_store["last"])
                at_tok = co.ld.put(at[:, :, :],
                                   xap(Xin if e == 0 else AACC, si, tt))
                co.ld.wait(v, at_tok)
                v.wait_ge(co.mm.h, mkb)
                mb_mark = co.vop(v.tensor_copy, mbs[:, :], psm)
                for dc in range(DC):
                    pb = dc % 2
                    if dvy.get(pb):
                        t.wait_ge(co.dv.h, dvy[pb])
                    for fc in range(FC):
                        ins = t.matmul(psy[:, pb, :],
                                       w3[:, fc, dc * 128:(dc + 1) * 128],
                                       sbt[:, fc, :], start=(fc == 0),
                                       stop=(fc == FC - 1))
                    mmk = co.mm.inc(ins)
                    v.wait_ge(co.mm.h, mmk)
                    co.vop(v.scalar_tensor_tensor, ytm[:, :], psy[:, pb, :],
                           b3[:, dc:dc + 1], mbs[:, :], ALU.add, ALU.mult)
                    dvy[pb] = co.vop(v.tensor_tensor, at[:, dc, :], at[:, dc, :],
                                     ytm[:, :], ALU.add)
                co.dv.wait(y)
                a_store[(si, tt)] = a_store["last"] = co.ld.put(
                    xap(Xout if e == NE - 1 else AACC, si, tt), at[:, :, :])


NH = 16
HD = 64
NQT = SEQ // 128


def emit_attn(co, C, Xin, Xout, scratch, cfg):
    nc = co.nc
    t, s, v, g, y = co.t, co.s, co.v, co.g, co.y
    kind = cfg["kind"]
    causal = cfg["causal"]
    QSP, CTX = scratch["QSP"], scratch["CTX"]

    with (
        nc.sbuf_tensor(co.u("Ks"), [128, DC, SEQ], BF16) as Ks,
        nc.sbuf_tensor(co.u("Vt"), [128, NQT, DM], BF16) as Vt,
        nc.sbuf_tensor(co.u("Pb"), [128, SEQ], BF16) as Pb,
        nc.sbuf_tensor(co.u("PTb"), [128, NQT, 512], BF16) as PT,
        nc.sbuf_tensor(co.u("rwb"), [128, 8], F32) as rwb,
        nc.sbuf_tensor(co.u("cxs"), [128, TT], BF16) as cxs,
        nc.psum_tensor(co.u("ps8"), [128, 8, TT], F32) as ps8,
    ):
        ps_sc = ps8[:, 0:4, :].rearrange("p a n -> p (a n)")
        ps_ptb = ps8[:, 4, :].bitcast(BF16)
        ps_cx = ps8[0:64, 5, :]
        ps_pj = ps8[:, 6:8, :]
        ps_rp = ps8[:, 5, :]
        ps_lst = ps8[0:1, 0:2, :]
        ps_lbc = ps8[:, 2:4, :]
        for si in range(NSEQ):
            with ExitStack() as _st:
                xts = _st.enter_context(nc.sbuf_tensor(co.u("a_xts"), [128, DC, TT], F32))
                xnt = _st.enter_context(nc.sbuf_tensor(co.u("a_xnt"), [128, DC, TT], BF16))
                xb = _st.enter_context(nc.sbuf_tensor(co.u("a_xb"), [128, DC, TT], BF16))
                sq = _st.enter_context(nc.sbuf_tensor(co.u("a_sq"), [128, DC, TT], BF16))
                rows = _st.enter_context(nc.sbuf_tensor(co.u("a_rws"), [1, 3, TT], F32))
                Qtile = _st.enter_context(nc.sbuf_tensor(co.u("a_qt"), [128, DC, TT], BF16))
                qraw = _st.enter_context(nc.sbuf_tensor(co.u("a_qr"), [128, TT], BF16))
                tr1 = _st.enter_context(nc.sbuf_tensor(co.u("a_t1"), [128, TT], BF16))
                wqkv = _st.enter_context(nc.sbuf_tensor(co.u("a_wq"), [128, DC, 3 * DM], BF16))
                bs = _st.enter_context(nc.sbuf_tensor(co.u("a_bs"), [128, 24], F32))
                rc2 = _st.enter_context(nc.sbuf_tensor(co.u("a_rc"), [128, 2, TT], F32))
                rc3 = _st.enter_context(nc.sbuf_tensor(co.u("a_r3"), [128, 2, DC], F32))
                bvb = _st.enter_context(nc.sbuf_tensor(co.u("a_bv"), [128, DM], F32))
                wt = []
                if kind == "l3":
                    for kc in range(DC):
                        wt.append(co.wq.put(wqkv[:, kc, :], cfg["wqkv"][kc]))
                else:
                    for nm, off in (("wq", 0), ("wk", DM), ("wv", 2 * DM)):
                        for kc in range(DC):
                            wt.append(co.wq.put(wqkv[:, kc, off:off + DM],
                                                cfg[nm][kc]))
                bt = []
                if kind == "l2":
                    bt.append(co.ld.put(bs[:, 0:8], cfg["bq"]))
                    bt.append(co.ld.put(bs[:, 8:16], cfg["bk"]))
                if kind == "l3":
                    bt.append(co.ld.put(bs[:, 0:24], cfg["bqkv"]))
                    bt.append(co.ld.put(rc3[:, 0, :], cfg["ropeC"]))
                    bt.append(co.ld.put(rc3[:, 1, :], cfg["ropeS"]))
                if kind in ("l2", "l3"):
                    bvt = co.ld.put(bvb[0:1, :], cfg["bv_row"])
                    co.ld.wait(t, bvt)
                    co.dv.wait(t)
                    co.ev.wait(t)
                    t.matmul(ps_pj[:, 0, :], C["onesrow_f"], bvb[0:1, 0:TT],
                             start=True, stop=True)
                    mk = co.mm.inc(t.matmul(ps_pj[:, 1, :], C["onesrow_f"],
                                            bvb[0:1, TT:2 * TT], start=True, stop=True))
                    v.wait_ge(co.mm.h, mk)
                    co.vop(v.tensor_copy, bvb[:, 0:TT], ps_pj[:, 0, :])
                    co.vop(v.tensor_copy, bvb[:, TT:2 * TT], ps_pj[:, 1, :])
                for tok in wt:
                    co.wq.wait(t, tok)
                for tok in bt:
                    co.ld.wait(s, tok)
                    co.ld.wait(v, tok)
                qst = None
                for tt in range(NTT):
                    lt = co.ld.put(xts[:, :, :], xap(Xin, si, tt))
                    co.ld.wait(s, lt)
                    co.ld.wait(v, lt)
                    emit_ln_tile(co, C, xts, xnt, ps_lst, ps_lbc, xb, sq, rows, TT)
                    if kind == "l2":
                        rt = co.ld.put(rc2[:, :, :],
                                       cfg["ropeCS"][:, :, tt * TT:(tt + 1) * TT]
                                       .rearrange("p a n -> p a n"))
                        co.ld.wait(v, rt)
                    co.dv.wait(t)
                    for which in (0, 1):
                        for oc in range(DC):
                            pb = oc % 2
                            co.ev.wait(t)
                            co.dv.wait(t)
                            for kc in range(DC):
                                ins = t.matmul(
                                    ps_pj[:, pb, :],
                                    wqkv[:, kc, which * DM + oc * 128:
                                         which * DM + (oc + 1) * 128],
                                    xnt[:, kc, :],
                                    start=(kc == 0), stop=(kc == DC - 1))
                            mk = co.mm.inc(ins)
                            dst = (Qtile[:, oc, :] if which == 0
                                   else Ks[:, oc, tt * TT:(tt + 1) * TT])
                            qsc = 0.125 if which == 0 else 1.0
                            if kind == "l1":
                                v.wait_ge(co.mm.h, mk)
                                co.vop(v.tensor_scalar_mul, dst,
                                       ps_pj[:, pb, :], qsc)
                                continue
                            bias = bs[:, which * 8 + oc:which * 8 + oc + 1]
                            v.wait_ge(co.mm.h, mk)
                            co.vop(v.tensor_scalar, qraw[:, :],
                                   ps_pj[:, pb, :], bias, qsc, ALU.add, ALU.mult)
                            co.dv.wait(t)
                            ins = t.matmul(ps_rp, C["prm_b"], qraw[:, :],
                                           start=True, stop=True)
                            mk2 = co.mm.inc(ins)
                            if kind == "l2":
                                cc, ss_ = rc2[:, 0, :], rc2[:, 1, :]

                                co.ev.wait(v)
                                co.vop(v.tensor_mul, tr1[:, :], qraw[:, :], cc)
                                v.wait_ge(co.mm.h, mk2)
                                co.vop(v.tensor_mul, qraw[:, :], ps_rp, ss_)
                                co.vop(v.tensor_add, dst, tr1[:, :], qraw[:, :])
                            else:
                                co.ev.wait(v)
                                co.vop(v.tensor_scalar_mul, tr1[:, :], qraw[:, :],
                                       rc3[:, 0, oc:oc + 1])
                                v.wait_ge(co.mm.h, mk2)
                                co.vop(v.tensor_scalar_mul, qraw[:, :], ps_rp,
                                       rc3[:, 1, oc:oc + 1])
                                co.vop(v.tensor_add, dst, tr1[:, :], qraw[:, :])
                    co.dv.wait(y)
                    co.ev.wait(y)
                    qst = co.ld.put(
                        QSP[:, :, tt * TT:(tt + 1) * TT].rearrange("c p n -> p c n"),
                        Qtile[:, :, :])
                    for tc in range(TT // 128):
                        tcg = tt * (TT // 128) + tc
                        co.dv.wait(t)
                        co.ev.wait(t)
                        for kc in range(DC):
                            for vc in range(2):
                                ins = t.matmul(
                                    ps_pj[:, vc, :],
                                    xnt[:, kc, tc * 128:(tc + 1) * 128],
                                    wqkv[:, kc, 2 * DM + vc * TT:2 * DM + (vc + 1) * TT],
                                    start=(kc == 0), stop=(kc == DC - 1))
                        mk = co.mm.inc(ins)
                        v.wait_ge(co.mm.h, mk)
                        for vc in range(2):
                            if kind == "l1":
                                co.vop(v.tensor_copy,
                                       Vt[:, tcg, vc * TT:(vc + 1) * TT],
                                       ps_pj[:, vc, :])
                            else:
                                co.vop(v.tensor_tensor,
                                       Vt[:, tcg, vc * TT:(vc + 1) * TT],
                                       ps_pj[:, vc, :],
                                       bvb[:, vc * TT:(vc + 1) * TT], ALU.add)
                    if qst is not None:
                        co.ld.wait(y, qst)
                    co.ev.wait(y)
                    co.dv.wait(y)
            co.join()
            with ExitStack() as _st:
                Qh = _st.enter_context(nc.sbuf_tensor(co.u("a_qh"), [128, SEQ], BF16))
                st_ = {"exp": {0: 0, 1: 0}, "rmx": {}, "nrm": {}, "ctx_ev": 0,
                       "sc": [ps_sc, ps_sc]}

                def nkc_of(qt):
                    return (qt + 1) if causal else NQT

                def emit_scores(h, qt):
                    hb, hc = (h % 2) * 64, h // 2
                    ngrp = (qt // 4 + 1) if causal else 4
                    psc = st_["sc"][qt % 2]
                    em = max(st_["exp"].values())
                    if em:
                        t.wait_ge(co.ev.h, em)
                    rm = max(st_["rmx"].values()) if st_["rmx"] else 0
                    if rm:
                        t.wait_ge(co.dv.h, rm)
                    for kg in range(ngrp):
                        ins = t.matmul(
                            psc[:, kg * TT:(kg + 1) * TT],
                            Qh[hb:hb + 64, qt * 128:(qt + 1) * 128],
                            Ks[hb:hb + 64, hc, kg * TT:(kg + 1) * TT],
                            start=True, stop=True)
                    st_["mks"] = co.mm.inc(ins)

                def emit_softmax(qt):
                    W = nkc_of(qt) * 128
                    psc = st_["sc"][qt % 2]
                    v.wait_ge(co.mm.h, st_["mks"])
                    st_["rmx"][qt % 2] = co.vop(v.reduce_max, rwb[:, 0:1],
                                                psc[:, 0:W], AX.X, negate=True)
                    s.wait_ge(co.dv.h, st_["rmx"][qt % 2])
                    st_["exp"][qt % 2] = co.sop(s.activation, Pb[:, 0:W],
                                                psc[:, 0:W],
                                                AF.Exp, bias=rwb[:, 0:1],
                                                accum_out=rwb[:, 2:3])
                    co.ev.wait(v)
                    if causal:
                        dg = Pb[:, qt * 128:(qt + 1) * 128]
                        co.vop(v.reduce_sum, rwb[:, 3:4], dg, AX.X)
                        co.vop(v.tensor_tensor, dg, dg, C["tril_b"], ALU.mult)
                        co.vop(v.reduce_sum, rwb[:, 4:5], dg, AX.X)
                        co.vop(v.tensor_sub, rwb[:, 2:3], rwb[:, 2:3], rwb[:, 3:4])
                        co.vop(v.tensor_add, rwb[:, 2:3], rwb[:, 2:3], rwb[:, 4:5])
                    co.vop(v.reciprocal, rwb[:, 6:7], rwb[:, 2:3])
                    st_["nrm"][qt] = co.vop(v.tensor_scalar_mul, Pb[:, 0:W],
                                            Pb[:, 0:W], rwb[:, 6:7])

                def emit_pt_av(h, qt):
                    hb, hc = (h % 2) * 64, h // 2
                    nkc = nkc_of(qt)
                    t.wait_ge(co.dv.h, st_["nrm"].pop(qt))
                    sub = (qt % 4) * 128
                    cp_marks = {}
                    for kc in range(nkc):
                        bk = 4 if kc % 2 == 0 else 6
                        view = ps8[:, bk, :].bitcast(BF16)[:, 0:128]
                        if kc >= 2:
                            sem, mk_ = cp_marks[kc - 2]
                            t.wait_ge(sem.h, mk_)
                        ins = t.transpose(view, Pb[:, kc * 128:(kc + 1) * 128],
                                          C["ident_b"])
                        mkt = co.mm.inc(ins)
                        if kc % 2 == 0:
                            v.wait_ge(co.mm.h, mkt)
                            cp_marks[kc] = (co.dv, co.vop(
                                v.tensor_copy, PT[:, kc, sub:sub + 128], view))
                        else:
                            s.wait_ge(co.mm.h, mkt)
                            cp_marks[kc] = (co.ev, co.sop(
                                s.copy, PT[:, kc, sub:sub + 128], view))
                    gmax = min(((qt // 4) * 4 + 4), NQT) if causal else NQT
                    if causal:
                        for kc in range(nkc, gmax):
                            co.vop(v.memset, PT[:, kc, sub:sub + 128], 0.0)
                    if qt % 4 == 3:
                        d_ = co.dv.count
                        t.wait_ge(co.dv.h, d_)
                        e_ = co.ev.count
                        t.wait_ge(co.ev.h, e_)
                        for kc in range(gmax):
                            ins = t.matmul(ps_cx[:, :],
                                           Vt[:, kc, h * HD:(h + 1) * HD],
                                           PT[:, kc, :],
                                           start=(kc == 0), stop=(kc == gmax - 1))
                        mka = co.mm.inc(ins)
                        s.wait_ge(co.mm.h, mka)
                        st_["ctx_ev"] = co.sop(s.copy, cxs[0:64, :], ps_cx[:, :])
                        y.wait_ge(co.ev.h, st_["ctx_ev"])
                        ct = co.ld.put(
                            CTX[hc, hb:hb + 64, (qt - 3) * 128:(qt + 1) * 128],
                            cxs[0:64, :])
                        co.ld.wait(s, ct)

                prev = None
                for h in range(NH):
                    hb = (h % 2) * 64
                    hc = h // 2
                    y.wait_ge(co.mm.h, co.mm.count)
                    qt_ld = co.ld.put(Qh[hb:hb + 64, :], QSP[hc, hb:hb + 64, :])
                    co.ld.wait(t, qt_ld)
                    for qt in range(NQT):
                        emit_scores(h, qt)
                        if prev is not None:
                            emit_pt_av(*prev)
                        emit_softmax(qt)
                        prev = (h, qt)
                emit_pt_av(*prev)
            co.join()
            with ExitStack() as _st:
                wo = _st.enter_context(nc.sbuf_tensor(co.u("a_wo"), [128, DC, DM], BF16))
                bo = _st.enter_context(nc.sbuf_tensor(co.u("a_bo"), [128, 8], F32))
                cxt = _st.enter_context(nc.sbuf_tensor(co.u("a_cx"), [128, DC, TT], BF16))
                xrt = _st.enter_context(nc.sbuf_tensor(co.u("a_xr"), [128, DC, TT], F32))
                wt = []
                if kind != "l1":
                    wt = [co.wq.put(wo[:, kc, :], cfg["wo"][kc]) for kc in range(DC)]
                    bt = co.ld.put(bo[:, :], cfg["bo"])
                    co.ld.wait(v, bt)
                pst = None
                for tt in range(NTT):
                    if pst is not None:
                        co.ld.wait(y, pst)
                    ct = co.ld.put(cxt[:, :, :],
                                   CTX[:, :, tt * TT:(tt + 1) * TT]
                                   .rearrange("c p n -> p c n"))
                    xt_ = co.ld.put(xrt[:, :, :], xap(Xin, si, tt))
                    if kind == "l1":
                        co.ld.wait(v, ct)
                        co.ld.wait(v, xt_)
                        for dc in range(DC):
                            co.vop(v.tensor_tensor, xrt[:, dc, :], xrt[:, dc, :],
                                   cxt[:, dc, :], ALU.add)
                    else:
                        if tt == 0:
                            for tok in wt:
                                co.wq.wait(t, tok)
                        co.ld.wait(t, ct)
                        co.ld.wait(v, xt_)
                        for oc in range(DC):
                            pb = oc % 2
                            co.dv.wait(t)
                            for kc in range(DC):
                                ins = t.matmul(ps_pj[:, pb, :],
                                               wo[:, kc, oc * 128:(oc + 1) * 128],
                                               cxt[:, kc, :],
                                               start=(kc == 0), stop=(kc == DC - 1))
                            mk = co.mm.inc(ins)
                            v.wait_ge(co.mm.h, mk)
                            co.vop(v.scalar_tensor_tensor, xrt[:, oc, :],
                                   ps_pj[:, pb, :], bo[:, oc:oc + 1],
                                   xrt[:, oc, :], ALU.add, ALU.add)
                    co.dv.wait(y)
                    pst = co.ld.put(xap(Xout, si, tt), xrt[:, :, :])
            co.join()


def _t(a):
    return np.ascontiguousarray(np.asarray(a).T)


def _chunk_k(a, name):
    a = np.ascontiguousarray(a)
    K, M = a.shape
    return a.reshape(K // 128, 128, M)


def _bias_cols(b):
    b = np.asarray(b)
    return np.ascontiguousarray(b.reshape(-1, 128).T)


def make_consts():
    C = {}
    C["ones128_b"] = np.ones((128, 1), dtype=BF)
    C["onesrow_b"] = np.ones((1, 128), dtype=BF)
    C["onesrow_f"] = np.ones((1, 128), dtype=np.float32)
    C["eps1"] = np.full((1, 1), EPS, dtype=np.float32)
    C["c127i"] = np.full((128, 1), 1.0 / 127.0, dtype=np.float32)
    C["ident_b"] = np.eye(128, dtype=BF)
    prm = np.zeros((128, 128), dtype=BF)
    for b in (0, 64):
        for j in range(32):
            prm[b + 32 + j, b + j] = -1.0
            prm[b + j, b + 32 + j] = 1.0
    C["prm_b"] = prm
    C["ident_f"] = np.eye(128, dtype=np.float32)
    C["cm1"] = np.full((128, 1), -1.0, dtype=np.float32)
    C["c1"] = np.full((128, 1), 1.0, dtype=np.float32)
    C["c2pi"] = np.full((128, 1), 2.0 * np.pi, dtype=np.float32)
    C["cm2pi"] = np.full((128, 1), -2.0 * np.pi, dtype=np.float32)
    C["ones64_f"] = np.ones((64, 1), dtype=np.float32)
    C["tril_b"] = np.tril(np.ones((128, 128), dtype=np.float32)).astype(BF)
    C["c0125"] = np.full((128, 1), 0.125, dtype=np.float32)
    C["iota8"] = np.broadcast_to(np.arange(8, dtype=np.float32), (128, 8)).copy()
    sel = np.zeros((8, 8 * 128), dtype=np.float32)
    for e in range(8):
        sel[e, e * 128:(e + 1) * 128] = 1.0
    C["sel8"] = sel
    return C


def declare_consts(nc, C):
    out = {}
    for k, a in C.items():
        dt = BF16 if a.dtype == BF else F32
        out[k] = nc.dram_tensor("c_" + k, list(a.shape), dt, kind="ExternalInput").ap()
    return out


def load_consts(co, dC):
    nc = co.nc
    sb = {}
    toks = []
    specs = {
        "ones128_b": ([128, 1], BF16), "onesrow_b": ([1, 128], BF16),
        "onesrow_f": ([1, 128], F32), "eps1": ([1, 1], F32),
        "c127i": ([128, 1], F32), "ident_b": ([128, 128], BF16),
        "iota8": ([128, 8], F32), "sel8": ([8, 8 * 128], F32),
        "prm_b": ([128, 128], BF16), "c0125": ([128, 1], F32),
        "ident_f": ([128, 128], F32), "cm1": ([128, 1], F32),
        "c1": ([128, 1], F32), "c2pi": ([128, 1], F32),
        "cm2pi": ([128, 1], F32),
        "ones64_f": ([64, 1], F32), "tril_b": ([128, 128], BF16),
    }
    for k, (shape, dt) in specs.items():
        h = nc.alloc_sbuf_tensor("cs_" + k, shape, dt)
        sb[k] = h.ap()
        toks.append(co.ld.put(sb[k], dC[k]))
    for e in co.engines():
        co.ld.wait_all(e)
    return sb


def host_weights(inputs):
    W = {}
    W["l0f_w1"] = np.ascontiguousarray(np.asarray(inputs["L0_ffn_w_gate"]).T).reshape(DC, 128, 3072)
    W["l0f_wu"] = np.ascontiguousarray(np.asarray(inputs["L0_ffn_w_up"]).T).reshape(DC, 128, 3072)
    W["l0f_w2"] = np.ascontiguousarray(np.asarray(inputs["L0_ffn_w_down"]).T).reshape(24, 128, 1024)
    W["l1f_w1"] = np.ascontiguousarray(np.asarray(inputs["L1_ffn_w1"])).reshape(DC, 128, 4096)
    W["l1f_w2"] = np.ascontiguousarray(np.asarray(inputs["L1_ffn_w2"])).reshape(32, 128, 1024)
    W["l1f_b1"] = _bias_cols(inputs["L1_ffn_b1"])
    W["l1f_b2"] = _bias_cols(inputs["L1_ffn_b2"])
    for nm, k in (("wq", "L1_attn_w_Q"), ("wk", "L1_attn_w_K"), ("wv", "L1_attn_w_V")):
        W[f"l1a_{nm}"] = np.ascontiguousarray(np.asarray(inputs[k])).reshape(DC, 128, DM)
    for nm, k in (("wq", "L2_attn_Wq"), ("wk", "L2_attn_Wk"),
                  ("wv", "L2_attn_Wv"), ("wo", "L2_attn_Wo")):
        W[f"l2a_{nm}"] = np.ascontiguousarray(np.asarray(inputs[k])).reshape(DC, 128, DM)
    W["l2a_bq"] = _bias_cols(inputs["L2_attn_bq"])
    W["l2a_bk"] = _bias_cols(inputs["L2_attn_bk"])
    W["l2a_bo"] = _bias_cols(inputs["L2_attn_bo"])
    W["l2a_bv_row"] = np.ascontiguousarray(np.asarray(inputs["L2_attn_bv"]).reshape(1, DM))
    inv_freq = np.exp(np.arange(0, HD, 2, dtype=np.float32) * (-np.log(10000.0) / HD))
    ang = np.arange(SEQ, dtype=np.float32)[None, :] * inv_freq[:, None]
    cs = np.zeros((128, 2, SEQ), dtype=np.float32)
    for b in range(4):
        cs[b * 32:(b + 1) * 32, 0, :] = np.cos(ang)
        cs[b * 32:(b + 1) * 32, 1, :] = np.sin(ang)
    W["l2a_ropeCS"] = cs
    qkv = np.asarray(inputs["L3_attn_qkv_w"])
    perm = np.zeros(3 * DM, dtype=np.int64)
    for h in range(NH):
        for part in range(3):
            perm[part * DM + h * HD: part * DM + (h + 1) * HD] = \
                np.arange(h * 3 * HD + part * HD, h * 3 * HD + (part + 1) * HD)
    qkvP = np.ascontiguousarray(qkv[:, perm])
    W["l3a_wqkv"] = qkvP.reshape(DC, 128, 3 * DM)
    qkvbP = np.asarray(inputs["L3_attn_qkv_b"])[perm]
    W["l3a_bqkv"] = _bias_cols(qkvbP)
    W["l3a_bv_row"] = np.ascontiguousarray(qkvbP[2 * DM:].reshape(1, DM))
    W["l3a_wo"] = np.ascontiguousarray(np.asarray(inputs["L3_attn_out_w"])).reshape(DC, 128, DM)
    W["l3a_bo"] = _bias_cols(inputs["L3_attn_out_b"])
    inv3 = 1.0 / 10000.0 ** (np.arange(0, HD, 2, dtype=np.float32) / HD)
    c3 = np.zeros((128, DC), dtype=np.float32)
    s3 = np.zeros((128, DC), dtype=np.float32)
    for oc in range(DC):
        for hh in range(2):
            h = 2 * oc + hh
            fr = h * inv3
            col = np.concatenate([np.cos(fr), np.cos(fr)])
            sol = np.concatenate([np.sin(fr), np.sin(fr)])
            c3[hh * 64:(hh + 1) * 64, oc] = col
            s3[hh * 64:(hh + 1) * 64, oc] = sol
    W["l3a_ropeC"] = np.ascontiguousarray(c3)
    W["l3a_ropeS"] = np.ascontiguousarray(s3)
    W["l2f_router"] = np.ascontiguousarray(np.asarray(inputs["L2_ffn_router_w"]).T).reshape(DC, 128, 8)
    W["l2f_gateW"] = np.ascontiguousarray(np.asarray(inputs["L2_ffn_gate_w"]).transpose(0, 2, 1)).reshape(8, DC, 128, 2816)
    W["l2f_upW"] = np.ascontiguousarray(np.asarray(inputs["L2_ffn_up_w"]).transpose(0, 2, 1)).reshape(8, DC, 128, 2816)
    W["l2f_downW"] = np.ascontiguousarray(np.asarray(inputs["L2_ffn_down_w"]).transpose(0, 2, 1)).reshape(8, 22, 128, DM)
    W["l3f_router"] = np.ascontiguousarray(np.asarray(inputs["L3_ffn_gate_w"])).reshape(DC, 128, 8)
    W["l3f_gate_b_row"] = np.ascontiguousarray(np.asarray(inputs["L3_ffn_gate_b"]).reshape(1, 8))
    W["l3f_w1"] = np.ascontiguousarray(np.asarray(inputs["L3_ffn_w1"])).reshape(8, DC, 128, 2048)
    W["l3f_w2"] = np.ascontiguousarray(np.asarray(inputs["L3_ffn_w2"])).reshape(8, DC, 128, 2048)
    W["l3f_wg"] = np.ascontiguousarray(np.asarray(inputs["L3_ffn_wg"])).reshape(8, 16, 128, 2048)
    W["l3f_w3"] = np.ascontiguousarray(np.asarray(inputs["L3_ffn_w3"])).reshape(8, 16, 128, DM)
    W["l3f_b1"] = np.stack([_bias_cols(b) for b in np.asarray(inputs["L3_ffn_b1"])])
    W["l3f_b2"] = np.stack([_bias_cols(b) for b in np.asarray(inputs["L3_ffn_b2"])])
    W["l3f_bg"] = np.stack([_bias_cols(b) for b in np.asarray(inputs["L3_ffn_bg"])])
    W["l3f_b3"] = np.stack([_bias_cols(b) for b in np.asarray(inputs["L3_ffn_b3"])])
    W["l0a_wch"] = np.ascontiguousarray(np.asarray(inputs["L0_attn_W_charge"]).T).reshape(DC, 128, 1)
    W["l0a_bch"] = np.asarray(inputs["L0_attn_b_charge"], dtype=np.float32).reshape(1, 1)
    W["l0a_wsh"] = np.ascontiguousarray(np.asarray(inputs["L0_attn_W_shell"]).T).reshape(DC, 128, 64)
    W["l0a_bsh"] = np.ascontiguousarray(np.asarray(inputs["L0_attn_b_shell"], dtype=np.float32).reshape(64, 1))
    W["l0a_omT"] = np.ascontiguousarray(np.asarray(inputs["L0_attn_omega"]).T)
    W["l0a_rfb"] = np.ascontiguousarray(np.asarray(inputs["L0_attn_rff_bias"], dtype=np.float32).reshape(32, 1))
    wc = np.asarray(inputs["L0_attn_W_combine"])
    W["l0a_wcm"] = np.ascontiguousarray(
        np.stack([wc[:, m * 64:(m + 1) * 64].T for m in range(5)]))
    W["l0a_wv"] = np.ascontiguousarray(np.asarray(inputs["L0_attn_W_v"]).T).reshape(DC, 128, DM)
    W["l0a_bv_row"] = np.ascontiguousarray(np.asarray(inputs["L0_attn_b_v"]).reshape(1, DM))
    W["l0a_wo"] = np.ascontiguousarray(np.asarray(inputs["L0_attn_W_out"]).T).reshape(DC, 128, DM)
    W["l0a_bo"] = _bias_cols(inputs["L0_attn_b_out"])
    W["l0a_log_temp"] = np.asarray(inputs["L0_attn_log_temp"], dtype=np.float32).reshape(1, 1)
    idx = np.arange(SEQ, dtype=np.float32)
    Tfull = np.exp(-0.3 * np.abs(idx[:, None] - idx[None, :]) / 10.0).astype(np.float32)
    W["l0a_Tq"] = np.ascontiguousarray(Tfull.reshape(NQT, 128, SEQ))
    return W


def declare_dram(nc, W, names):
    d = {}
    for n in names:
        a = W[n]
        dt = {np.dtype(np.float32): F32, np.dtype(np.int32): I32}[a.dtype]
        d[n] = nc.dram_tensor(n, list(a.shape), dt, kind="ExternalInput").ap()
    return d


def _ffn_cfg(name, d):
    if name == "l0_ffn":
        return dict(kind="silu_mul", w1=d["l0f_w1"], wu=d["l0f_wu"],
                    w2=d["l0f_w2"], b1=None, b2=None, H=3072, scale=0.5,
                    out_mul=1.0 / (127.0 * 127.0))
    if name == "l1_ffn":
        return dict(kind="gelu", w1=d["l1f_w1"], w2=d["l1f_w2"],
                    b1=d["l1f_b1"], b2=d["l1f_b2"], H=4096, scale=0.5)
    raise KeyError(name)


def _attn_cfg(name, d):
    if name == "l1_attn":
        return dict(kind="l1", causal=False, wq=d["l1a_wq"], wk=d["l1a_wk"],
                    wv=d["l1a_wv"])
    if name == "l2_attn":
        return dict(kind="l2", causal=True, wq=d["l2a_wq"], wk=d["l2a_wk"],
                    wv=d["l2a_wv"], wo=d["l2a_wo"], bq=d["l2a_bq"],
                    bk=d["l2a_bk"], bo=d["l2a_bo"], bv_row=d["l2a_bv_row"],
                    ropeCS=d["l2a_ropeCS"])
    if name == "l3_attn":
        return dict(kind="l3", causal=False, wqkv=d["l3a_wqkv"],
                    bqkv=d["l3a_bqkv"], bv_row=d["l3a_bv_row"],
                    wo=d["l3a_wo"], bo=d["l3a_bo"],
                    ropeC=d["l3a_ropeC"], ropeS=d["l3a_ropeS"])
    raise KeyError(name)


def _attn0_cfg(d):
    coeffs = np.asarray([(-1.0) ** m / math.factorial(m) for m in range(5)],
                        dtype=np.float32)
    mags = np.sqrt(np.abs(coeffs) + 1e-10).astype(np.float32)
    return dict(wch=d["l0a_wch"], bch=d["l0a_bch"], wsh=d["l0a_wsh"],
                bsh=d["l0a_bsh"], omT=d["l0a_omT"], rfb=d["l0a_rfb"],
                wcm=d["l0a_wcm"], wv=d["l0a_wv"], bv_row=d["l0a_bv_row"],
                wo=d["l0a_wo"], bo=d["l0a_bo"], log_temp=d["l0a_log_temp"],
                Tq=d["l0a_Tq"], mags=[float(m) for m in mags])


def _moe_cfg(name, d):
    if name == "l2_ffn":
        return dict(kind="l2", router=d["l2f_router"], gateW=d["l2f_gateW"],
                    upW=d["l2f_upW"], downW=d["l2f_downW"])
    if name == "l3_ffn":
        return dict(kind="l3", router=d["l3f_router"],
                    gate_b_row=d["l3f_gate_b_row"], w1=d["l3f_w1"],
                    w2=d["l3f_w2"], wg=d["l3f_wg"], w3=d["l3f_w3"],
                    b1=d["l3f_b1"], b2=d["l3f_b2"], bg=d["l3f_bg"],
                    b3=d["l3f_b3"])
    raise KeyError(name)


PART_WEIGHTS = {
    "l0_ffn": ["l0f_w1", "l0f_wu", "l0f_w2"],
    "l1_ffn": ["l1f_w1", "l1f_w2", "l1f_b1", "l1f_b2"],
    "l1_attn": ["l1a_wq", "l1a_wk", "l1a_wv"],
    "l2_attn": ["l2a_wq", "l2a_wk", "l2a_wv", "l2a_wo", "l2a_bq", "l2a_bk",
                "l2a_bo", "l2a_bv_row", "l2a_ropeCS"],
    "l3_attn": ["l3a_wqkv", "l3a_bqkv", "l3a_bv_row", "l3a_wo", "l3a_bo",
                "l3a_ropeC", "l3a_ropeS"],
    "l2_ffn": ["l2f_router", "l2f_gateW", "l2f_upW", "l2f_downW"],
    "l0_attn": ["l0a_wch", "l0a_bch", "l0a_wsh", "l0a_bsh", "l0a_omT",
                "l0a_rfb", "l0a_wcm", "l0a_wv", "l0a_bv_row", "l0a_wo",
                "l0a_bo", "l0a_log_temp", "l0a_Tq"],
    "l3_ffn": ["l3f_router", "l3f_gate_b_row", "l3f_w1", "l3f_w2", "l3f_wg",
               "l3f_w3", "l3f_b1", "l3f_b2", "l3f_bg", "l3f_b3"],
}


def emit_part(co, C, name, d, Xin, Xout, scratch):
    if name in ("l0_ffn", "l1_ffn"):
        emit_ffn(co, C, Xin, Xout, scratch["XN"], _ffn_cfg(name, d))
    elif name in ("l1_attn", "l2_attn", "l3_attn"):
        emit_attn(co, C, Xin, Xout, scratch, _attn_cfg(name, d))
    elif name in ("l2_ffn", "l3_ffn"):
        emit_moe(co, C, Xin, Xout, scratch, _moe_cfg(name, d))
    elif name == "l0_attn":
        emit_attn_l0(co, C, Xin, Xout, scratch, _attn0_cfg(d))
    else:
        raise KeyError(name)


def build_program(parts):
    import concourse.nx_utils as _nxu
    _nxu.display_graph = lambda G: None
    nc = bacc.Bacc("TRN2", target_bir_lowering=False)
    consts = make_consts()
    dC = declare_consts(nc, consts)
    wnames = sorted({w for p in parts for w in PART_WEIGHTS[p]})
    Whost = None
    x_in = nc.dram_tensor("x_in", [NSEQ, DC, 128, SEQ], F32, kind="ExternalInput").ap()
    x_out = nc.dram_tensor("x_out", [NSEQ, DC, 128, SEQ], F32, kind="ExternalOutput").ap()
    XS = [nc.dram_tensor(f"xs{i}", [NSEQ, DC, 128, SEQ], F32, kind="Internal").ap()
          for i in range(2)]
    scratch = {
        "XN": nc.dram_tensor("xn_sp", [NSEQ, DC, 128, SEQ], BF16, kind="Internal").ap(),
        "QSP": nc.dram_tensor("q_sp", [DC, 128, SEQ], BF16, kind="Internal").ap(),
        "CTX": nc.dram_tensor("ctx_sp", [DC, 128, SEQ], BF16, kind="Internal").ap(),
        "AACC": nc.dram_tensor("a_sp", [NSEQ, DC, 128, SEQ], F32, kind="Internal").ap(),
        "SSP": nc.dram_tensor("s_sp", [NSEQ, 16, 128, SEQ], BF16, kind="Internal").ap(),
    }
    dW = {}
    for n in wnames:
        shape, dt = DRAM_SHAPES[n]
        dW[n] = nc.dram_tensor(n, list(shape), dt, kind="ExternalInput").ap()
    co = Core(nc)
    C = load_consts(co, dC)
    for k, name in enumerate(parts):
        Xin = x_in if k == 0 else XS[k % 2]
        Xout = x_out if k == len(parts) - 1 else XS[(k + 1) % 2]
        emit_part(co, C, name, dW, Xin, Xout, scratch)
    nc.finalize()
    return nc


DRAM_SHAPES = {
    "l0f_w1": ((DC, 128, 3072), I32), "l0f_wu": ((DC, 128, 3072), I32),
    "l0f_w2": ((24, 128, 1024), I32),
    "l1f_w1": ((DC, 128, 4096), F32), "l1f_w2": ((32, 128, 1024), F32),
    "l1f_b1": ((128, 32), F32), "l1f_b2": ((128, 8), F32),
    "l1a_wq": ((DC, 128, DM), F32), "l1a_wk": ((DC, 128, DM), F32),
    "l1a_wv": ((DC, 128, DM), F32),
    "l2a_wq": ((DC, 128, DM), F32), "l2a_wk": ((DC, 128, DM), F32),
    "l2a_wv": ((DC, 128, DM), F32), "l2a_wo": ((DC, 128, DM), F32),
    "l2a_bq": ((128, 8), F32), "l2a_bk": ((128, 8), F32),
    "l2a_bo": ((128, 8), F32), "l2a_bv_row": ((1, DM), F32),
    "l2a_ropeCS": ((128, 2, SEQ), F32),
    "l3a_wqkv": ((DC, 128, 3 * DM), F32), "l3a_bqkv": ((128, 24), F32),
    "l3a_bv_row": ((1, DM), F32), "l3a_wo": ((DC, 128, DM), F32),
    "l3a_bo": ((128, 8), F32),
    "l3a_ropeC": ((128, DC), F32), "l3a_ropeS": ((128, DC), F32),
    "l2f_router": ((DC, 128, 8), F32),
    "l2f_gateW": ((8, DC, 128, 2816), F32), "l2f_upW": ((8, DC, 128, 2816), F32),
    "l2f_downW": ((8, 22, 128, DM), F32),
    "l3f_router": ((DC, 128, 8), F32), "l3f_gate_b_row": ((1, 8), F32),
    "l3f_w1": ((8, DC, 128, 2048), F32), "l3f_w2": ((8, DC, 128, 2048), F32),
    "l3f_wg": ((8, 16, 128, 2048), F32), "l3f_w3": ((8, 16, 128, DM), F32),
    "l3f_b1": ((8, 128, 16), F32), "l3f_b2": ((8, 128, 16), F32),
    "l3f_bg": ((8, 128, 16), F32), "l3f_b3": ((8, 128, 8), F32),
    "l0a_wch": ((DC, 128, 1), F32), "l0a_bch": ((1, 1), F32),
    "l0a_wsh": ((DC, 128, 64), F32), "l0a_bsh": ((64, 1), F32),
    "l0a_omT": ((64, 32), F32), "l0a_rfb": ((32, 1), F32),
    "l0a_wcm": ((5, 64, 64), F32), "l0a_wv": ((DC, 128, DM), F32),
    "l0a_bv_row": ((1, DM), F32), "l0a_wo": ((DC, 128, DM), F32),
    "l0a_bo": ((128, 8), F32), "l0a_log_temp": ((1, 1), F32),
    "l0a_Tq": ((NQT, 128, SEQ), F32),
}

ALL_PARTS = [
    "l0_ffn", "l0_attn", "l0_ffn",
    "l1_ffn", "l1_attn", "l1_ffn",
    "l2_ffn", "l2_attn", "l2_ffn",
    "l3_ffn", "l3_attn", "l3_ffn",
]

FUSED = True
_PROG_CACHE = {}


def _get_program(parts):
    key = tuple(parts)
    if key not in _PROG_CACHE:
        _PROG_CACHE[key] = build_program(list(parts))
    return _PROG_CACHE[key]


def shard_x(x):
    xt = np.ascontiguousarray(np.asarray(x, dtype=np.float32).transpose(0, 2, 1))
    xt = xt.reshape(16, DC, 128, SEQ)
    return [np.ascontiguousarray(xt[2 * c:2 * c + 2]) for c in range(NCORES)]


def unshard_x(shards):
    full = np.concatenate([s for s in shards], axis=0)
    full = full.reshape(16, DM, SEQ).transpose(0, 2, 1)
    return np.ascontiguousarray(full)


def run_parts(parts_groups, inputs, x):
    consts = make_consts()
    W = host_weights(inputs)
    xs = shard_x(x)
    for parts in parts_groups:
        nc = _get_program(tuple(parts))
        wnames = sorted({w for p in parts for w in PART_WEIGHTS[p]})
        in_maps = []
        for c in range(NCORES):
            m = {"x_in": xs[c]}
            for n in wnames:
                m[n] = W[n]
            for k, a in consts.items():
                m["c_" + k] = np.asarray(a)
            in_maps.append(m)
        res = run_bass_kernel_spmd(nc, in_maps, core_ids=list(range(NCORES)))
        xs = [res.results[c]["x_out"] for c in range(NCORES)]
    return unshard_x(xs)


def kernel(**inputs):
    x = np.asarray(inputs["L0_attn_x"], dtype=np.float32)
    if FUSED:
        groups = [ALL_PARTS]
    else:
        groups = [[p] for p in ALL_PARTS]
    return run_parts(groups, inputs, x)


# version 09 (final)
# speedup vs baseline: 1.1789x; 1.1789x over previous
import math
from contextlib import ExitStack
import numpy as np
import ml_dtypes
import concourse.bass as bass
import concourse.bacc as bacc
import concourse.mybir as mybir
from concourse.bass_utils import run_bass_kernel_spmd

F32 = mybir.dt.float32
BF16 = mybir.dt.bfloat16
I32 = mybir.dt.int32
AF = mybir.ActivationFunctionType
ALU = mybir.AluOpType
AX = mybir.AxisListType
BF = ml_dtypes.bfloat16

NCORES = 8
SEQ = 2048
NSEQ = 2
DM = 1024
DC = DM // 128
TT = 512
NTT = SEQ // TT
EPS = 1e-5


class Sem:
    def __init__(self, nc, name):
        self.h = nc.alloc_semaphore(name)
        self.count = 0

    def inc(self, instr, n=1):
        instr.then_inc(self.h, n)
        self.count += n
        return self.count

    def wait(self, eng, count=None):
        c = self.count if count is None else count
        if c > 0:
            eng.wait_ge(self.h, c)


class DmaQ:

    def __init__(self, nc, eng, name, nlanes=6):
        self.eng = eng
        self.lanes = [Sem(nc, f"{name}{i}") for i in range(nlanes)]
        self.i = 0

    def put(self, out, in_, **kw):
        lane = self.lanes[self.i % len(self.lanes)]
        self.i += 1
        ins = self.eng.dma_start(out=out, in_=in_, **kw)
        cnt = lane.inc(ins, 16)
        return (lane, cnt)

    def wait(self, eng, tok):
        lane, _ = tok
        eng.wait_ge(lane.h, lane.count)

    def wait_all(self, eng):
        for lane in self.lanes:
            lane.wait(eng)


class Core:
    def __init__(self, nc):
        self.nc = nc
        self.t = nc.tensor
        self.s = nc.scalar
        self.v = nc.vector
        self.g = nc.gpsimd
        self.y = nc.sync
        self.ld = DmaQ(nc, self.y, "ld", 6)
        self.wq = DmaQ(nc, self.g, "wq", 6)
        self.mm = Sem(nc, "mm")
        self.ev = Sem(nc, "ev")
        self.dv = Sem(nc, "dv")
        self.pl = Sem(nc, "pl")
        self.jn = Sem(nc, "jn")
        self._jk = 0
        self._uid = 0

    def u(self, name):
        self._uid += 1
        return f"{name}_{self._uid}"

    def engines(self):
        return [self.t, self.s, self.v, self.g, self.y]

    def vop(self, f, *a, **k):
        self.dv.wait(self.v)
        ins = f(*a, **k)
        return self.dv.inc(ins)

    def sop(self, f, *a, **k):
        self.ev.wait(self.s)
        ins = f(*a, **k)
        return self.ev.inc(ins)

    def gop(self, f, *a, **k):
        self.pl.wait(self.g)
        ins = f(*a, **k)
        return self.pl.inc(ins)

    def join(self):
        self.ld.wait_all(self.y)
        self.wq.wait_all(self.g)
        self.ev.wait(self.s)
        self.dv.wait(self.v)
        self.pl.wait(self.g)
        self._jk += 1
        for e in self.engines():
            e.nop().then_inc(self.jn.h, 1)
        self.jn.count += 5
        for e in self.engines():
            e.wait_ge(self.jn.h, self.jn.count)


def xap(X, si, tt, ntok=TT):
    return X[si, :, :, tt * ntok:(tt + 1) * ntok].rearrange("c p n -> p c n")


def emit_ln_tile(co, C, xt, xn_out, ps_st, ps_bc, xb, sq, rows, n):
    t, s, v = co.t, co.s, co.v
    co.mm.wait(s)
    co.sop(s.copy, xb.rearrange("p c n -> p (c n)"), xt.rearrange("p c n -> p (c n)"))
    e0 = co.sop(s.square, sq.rearrange("p c n -> p (c n)"),
                xt.rearrange("p c n -> p (c n)"))
    t.wait_ge(co.ev.h, e0)
    co.dv.wait(t)
    for dc in range(DC):
        t.matmul(ps_st[0:1, 0, :], C["ones128_b"], xb[:, dc, :],
                 start=(dc == 0), stop=(dc == DC - 1))
    for dc in range(DC):
        ins = t.matmul(ps_st[0:1, 1, :], C["ones128_b"], sq[:, dc, :],
                       start=(dc == 0), stop=(dc == DC - 1))
    m0 = co.mm.inc(ins)
    v.wait_ge(co.mm.h, m0)
    co.vop(v.tensor_scalar_mul, rows[:, 0, :n], ps_st[0:1, 0, :], 1.0 / DM)
    co.vop(v.tensor_scalar_mul, rows[:, 1, :n], ps_st[0:1, 1, :], 1.0 / DM)
    co.vop(v.tensor_mul, rows[:, 2, :n], rows[:, 0, :n], rows[:, 0, :n])
    d0 = co.vop(v.tensor_sub, rows[:, 1, :n], rows[:, 1, :n], rows[:, 2, :n])
    s.wait_ge(co.dv.h, d0)
    e1 = co.sop(s.activation, rows[:, 1, :n], rows[:, 1, :n], AF.Sqrt,
                bias=C["eps1"][0:1, 0:1])
    v.wait_ge(co.ev.h, e1)
    co.vop(v.reciprocal, rows[:, 1, :n], rows[:, 1, :n])
    d1 = co.vop(v.scalar_tensor_tensor, rows[:, 2, :n], rows[:, 0, :n], -1.0,
                rows[:, 1, :n], ALU.mult, ALU.mult)
    t.wait_ge(co.dv.h, d1)
    t.matmul(ps_bc[:, 0, :], C["onesrow_f"], rows[:, 1, :n], start=True, stop=True)
    m1 = co.mm.inc(t.matmul(ps_bc[:, 1, :], C["onesrow_f"], rows[:, 2, :n],
                            start=True, stop=True))
    v.wait_ge(co.mm.h, m1)
    for dc in range(DC):
        co.vop(v.tensor_mul, xn_out[:, dc, :], xt[:, dc, :], ps_bc[:, 0, :])
    for dc in range(DC):
        d = co.vop(v.tensor_tensor, xn_out[:, dc, :], xn_out[:, dc, :],
                   ps_bc[:, 1, :], ALU.add)
    return d


def emit_ln_pass(co, C, Xin, XN_out_hbm, xn_sbuf, ps6, bufs):
    s, v, y = co.s, co.v, co.y
    xt, xnt, xb, sq, rows = bufs
    ps_st = ps6[0:1, 0:2, :]
    ps_bc = ps6[:, 2:4, :]
    sttok = None
    for si in range(NSEQ):
        for tt in range(NTT):
            dst = xnt if xn_sbuf is None else \
                xn_sbuf[:, si, :, tt * TT:(tt + 1) * TT]
            lt = co.ld.put(xt[:, :, :], xap(Xin, si, tt))
            co.ld.wait(s, lt)
            co.ld.wait(v, lt)
            if sttok is not None and xn_sbuf is None:
                co.ld.wait(v, sttok)
            dtok = emit_ln_tile(co, C, xt, dst, ps_st, ps_bc, xb, sq, rows, TT)
            if XN_out_hbm is not None:
                y.wait_ge(co.dv.h, dtok)
                srcb = dst if xn_sbuf is not None else xnt
                sttok = co.ld.put(xap(XN_out_hbm, si, tt), srcb[:, :, :])
            co.ev.wait(y)
            co.dv.wait(y)
    return sttok


def emit_ffn(co, C, Xin, Xout, XN, w):
    nc = co.nc
    t, s, v, y = co.t, co.s, co.v, co.y
    H, HC = w["H"], w["H"] // 128
    kind = w["kind"]
    omul = w.get("out_mul", 1.0) * w["scale"]

    with (
        nc.sbuf_tensor(co.u("w1b"), [128, DC, H], BF16) as w1b,
        nc.sbuf_tensor(co.u("wub"), [128, DC, H if kind == "silu_mul" else 1], BF16) as wub,
        nc.sbuf_tensor(co.u("w2b"), [128, HC, DM], BF16) as w2b,
        nc.sbuf_tensor(co.u("b1s"), [128, HC], F32) as b1s,
        nc.sbuf_tensor(co.u("b2s"), [128, DC], F32) as b2s,
        nc.sbuf_tensor(co.u("xnt"), [128, DC, TT], BF16) as xnt,
        nc.sbuf_tensor(co.u("hbt"), [128, HC, TT], BF16) as hbt,
        nc.sbuf_tensor(co.u("xts"), [128, DC, TT], F32) as xts,
        nc.sbuf_tensor(co.u("gtmp"), [128, TT], BF16) as gtmp,
        nc.sbuf_tensor(co.u("ytmp"), [128, TT], F32) as ytmp,
        nc.sbuf_tensor(co.u("rows"), [1, 3, TT], F32) as rows,
        nc.psum_tensor(co.u("ps6"), [128, 6, TT], F32) as ps6,
    ):
        assert HC >= 2 * DC
        wtoks = [co.wq.put(w1b[:, kc, :], w["w1"][kc]) for kc in range(DC)]
        if kind == "silu_mul":
            wtoks += [co.wq.put(wub[:, kc, :], w["wu"][kc]) for kc in range(DC)]
        wtoks += [co.wq.put(w2b[:, hc, :], w["w2"][hc]) for hc in range(HC)]
        btoks = []
        if w.get("b1") is not None:
            btoks.append(co.ld.put(b1s[:, :], w["b1"]))
        if w.get("b2") is not None:
            btoks.append(co.ld.put(b2s[:, :], w["b2"]))

        ln_sttok = emit_ln_pass(co, C, Xin, XN, None, ps6,
                                (xts, xnt, hbt[:, 0:DC, :], hbt[:, DC:2 * DC, :], rows))

        for tok in wtoks:
            co.wq.wait(t, tok)
        for tok in btoks:
            co.ld.wait(s, tok)
            co.ld.wait(v, tok)
        co.dv.wait(t)

        psh = ps6[:, 0:4, :]
        psy = ps6[:, 4:6, :]
        prev_store = None
        prev_ytile_mm = 0
        prev_h_mark = 0
        ev_h_marks = {}
        dv_h_marks = {}
        dv_y_marks = {}
        for si in range(NSEQ):
            for tt in range(NTT):
                y.wait_ge(co.mm.h, prev_h_mark)
                if si == 0 and tt == 0 and ln_sttok is not None:
                    co.ld.wait(y, ln_sttok)
                xtok = co.ld.put(xnt[:, :, :], xap(XN, si, tt))
                co.ld.wait(t, xtok)

                for hc in range(HC):
                    if kind == "gelu":
                        pb = hc % 4
                        if ev_h_marks.get(pb):
                            t.wait_ge(co.ev.h, ev_h_marks[pb])
                        for kc in range(DC):
                            ins = t.matmul(psh[:, pb, :],
                                           w1b[:, kc, hc * 128:(hc + 1) * 128],
                                           xnt[:, kc, :],
                                           start=(kc == 0), stop=(kc == DC - 1))
                        mmk = co.mm.inc(ins)
                        s.wait_ge(co.mm.h, max(mmk, prev_ytile_mm))
                        ev_h_marks[pb] = co.sop(
                            s.activation, hbt[:, hc, :], psh[:, pb, :], AF.Gelu,
                            bias=b1s[:, hc:hc + 1])
                    else:
                        pb = (hc % 2) * 2
                        if dv_h_marks.get(pb):
                            t.wait_ge(co.dv.h, dv_h_marks[pb])
                        for kc in range(DC):
                            t.matmul(psh[:, pb, :],
                                     w1b[:, kc, hc * 128:(hc + 1) * 128],
                                     xnt[:, kc, :],
                                     start=(kc == 0), stop=(kc == DC - 1))
                        for kc in range(DC):
                            ins = t.matmul(psh[:, pb + 1, :],
                                           wub[:, kc, hc * 128:(hc + 1) * 128],
                                           xnt[:, kc, :],
                                           start=(kc == 0), stop=(kc == DC - 1))
                        mmk = co.mm.inc(ins)
                        s.wait_ge(co.mm.h, mmk)
                        if dv_h_marks.get("g"):
                            s.wait_ge(co.dv.h, dv_h_marks["g"])
                        e = co.sop(s.activation, gtmp[:, :], psh[:, pb, :], AF.Silu,
                                   scale=C["c127i"][:, 0:1])
                        v.wait_ge(co.ev.h, e)
                        if prev_ytile_mm:
                            v.wait_ge(co.mm.h, prev_ytile_mm)
                        dv_h_marks[pb] = dv_h_marks["g"] = co.vop(
                            v.tensor_mul, hbt[:, hc, :], gtmp[:, :], psh[:, pb + 1, :])

                prev_h_mark = co.mm.count
                if prev_store is not None:
                    co.ld.wait(y, prev_store)
                xrt = co.ld.put(xts[:, :, :], xap(Xin, si, tt))
                co.ld.wait(v, xrt)
                if kind == "gelu":
                    t.wait_ge(co.ev.h, co.ev.count)
                else:
                    t.wait_ge(co.dv.h, co.dv.count)
                for dc in range(DC):
                    pb = dc % 2
                    if dv_y_marks.get(pb):
                        t.wait_ge(co.dv.h, dv_y_marks[pb])
                    for hc in range(HC):
                        ins = t.matmul(psy[:, pb, :],
                                       w2b[:, hc, dc * 128:(dc + 1) * 128],
                                       hbt[:, hc, :],
                                       start=(hc == 0), stop=(hc == HC - 1))
                    mmk = co.mm.inc(ins)
                    v.wait_ge(co.mm.h, mmk)
                    if prev_store is not None and dc == 0:
                        co.ld.wait(v, prev_store)
                    if w.get("b2") is not None:
                        co.vop(v.tensor_scalar, ytmp[:, :], psy[:, pb, :],
                               b2s[:, dc:dc + 1], omul, ALU.add, ALU.mult)
                    else:
                        co.vop(v.tensor_scalar, ytmp[:, :], psy[:, pb, :], omul,
                               None, ALU.mult)
                    dv_y_marks[pb] = co.vop(v.tensor_tensor, xts[:, dc, :],
                                            xts[:, dc, :], ytmp[:, :], ALU.add)
                prev_ytile_mm = co.mm.count
                y.wait_ge(co.dv.h, co.dv.count)
                prev_store = co.ld.put(xap(Xout, si, tt), xts[:, :, :])
    co.join()


def emit_attn_l0(co, C, Xin, Xout, scratch, cfg):
    nc = co.nc
    t, s, v, g, y = co.t, co.s, co.v, co.g, co.y
    RFF = 32
    RS2 = (2.0 / (2 * RFF)) * math.exp(-0.5)

    with ExitStack() as _st:
        Vt = _st.enter_context(nc.sbuf_tensor(co.u("z_vt"), [128, NQT, DM], BF16))
        PHI = _st.enter_context(nc.sbuf_tensor(co.u("z_phi"), [64, SEQ], BF16))
        CXs = _st.enter_context(nc.sbuf_tensor(co.u("z_cx"), [128, DC, SEQ], BF16))
        Pb = _st.enter_context(nc.sbuf_tensor(co.u("z_pb"), [128, SEQ], BF16))
        PT = _st.enter_context(nc.sbuf_tensor(co.u("z_pt"), [128, NQT, 128], BF16))
        scf = _st.enter_context(nc.sbuf_tensor(co.u("z_sc"), [128, SEQ], F32))
        Tt = _st.enter_context(nc.sbuf_tensor(co.u("z_tt"), [128, SEQ], F32))
        rwb = _st.enter_context(nc.sbuf_tensor(co.u("z_rw"), [128, 8], F32))
        ivt = _st.enter_context(nc.sbuf_tensor(co.u("z_iv"), [128, 1], F32))
        ps8 = _st.enter_context(nc.psum_tensor(co.u("z_ps"), [128, 8, TT], F32))
        lt1 = _st.enter_context(nc.sbuf_tensor(co.u("z_lt"), [1, 8], F32))
        ltok = co.ld.put(lt1[0:1, 0:1], cfg["log_temp"])
        co.ld.wait(v, ltok)
        co.ld.wait(s, ltok)
        co.vop(v.tensor_scalar_mul, lt1[0:1, 1:2], lt1[0:1, 0:1], -1.0)
        co.vop(v.tensor_max, lt1[0:1, 1:2], lt1[0:1, 1:2], lt1[0:1, 0:1])
        d0 = co.dv.count
        s.wait_ge(co.dv.h, d0)
        co.sop(s.activation, lt1[0:1, 2:3], lt1[0:1, 1:2], AF.Exp,
               scale=C["cm1"][0:1, 0:1])
        co.sop(s.activation, lt1[0:1, 3:4], lt1[0:1, 2:3], AF.Ln,
               bias=C["c1"][0:1, 0:1])
        v.wait_ge(co.ev.h, co.ev.count)
        co.vop(v.tensor_single_scalar, lt1[0:1, 4:5], lt1[0:1, 0:1], 0.0, ALU.max)
        co.vop(v.tensor_add, lt1[0:1, 4:5], lt1[0:1, 4:5], lt1[0:1, 3:4])
        co.vop(v.tensor_scalar_add, lt1[0:1, 4:5], lt1[0:1, 4:5], 0.1)
        co.vop(v.reciprocal, lt1[0:1, 5:6], lt1[0:1, 4:5])
        co.vop(v.tensor_scalar_mul, lt1[0:1, 5:6], lt1[0:1, 5:6], RS2)
        d1 = co.dv.count
        t.wait_ge(co.dv.h, d1)
        mk = co.mm.inc(t.matmul(ps8[:, 7, 0:1], C["onesrow_f"], lt1[0:1, 5:6],
                                start=True, stop=True))
        v.wait_ge(co.mm.h, mk)
        co.vop(v.tensor_copy, ivt[:, :], ps8[:, 7, 0:1])
        for si in range(NSEQ):
            with ExitStack() as _s2:
                xts = _s2.enter_context(nc.sbuf_tensor(co.u("z_xt"), [128, DC, TT], F32))
                xnt = _s2.enter_context(nc.sbuf_tensor(co.u("z_xn"), [128, DC, TT], BF16))
                xb = _s2.enter_context(nc.sbuf_tensor(co.u("z_xb"), [128, DC, TT], BF16))
                sq = _s2.enter_context(nc.sbuf_tensor(co.u("z_sq"), [128, DC, TT], BF16))
                rows = _s2.enter_context(nc.sbuf_tensor(co.u("z_rs"), [1, 3, TT], F32))
                wv = _s2.enter_context(nc.sbuf_tensor(co.u("z_wv"), [128, DC, DM], BF16))
                wch = _s2.enter_context(nc.sbuf_tensor(co.u("z_wc"), [128, DC, 1], BF16))
                wsh = _s2.enter_context(nc.sbuf_tensor(co.u("z_ws"), [128, DC, 64], BF16))
                omT = _s2.enter_context(nc.sbuf_tensor(co.u("z_om"), [64, RFF], BF16))
                wcm = _s2.enter_context(nc.sbuf_tensor(co.u("z_wm"), [64, 5, 64], BF16))
                bsh = _s2.enter_context(nc.sbuf_tensor(co.u("z_bs"), [64, 1], F32))
                rfb = _s2.enter_context(nc.sbuf_tensor(co.u("z_rf"), [32, 1], F32))
                bvb = _s2.enter_context(nc.sbuf_tensor(co.u("z_bv"), [128, DM], F32))
                shl = _s2.enter_context(nc.sbuf_tensor(co.u("z_sh"), [64, TT], F32))
                ssq = _s2.enter_context(nc.sbuf_tensor(co.u("z_s2"), [64, TT], F32))
                shn = _s2.enter_context(nc.sbuf_tensor(co.u("z_sn"), [64, TT], BF16))
                prj = _s2.enter_context(nc.sbuf_tensor(co.u("z_pj"), [32, 2, TT], F32))
                phs = _s2.enter_context(nc.sbuf_tensor(co.u("z_ph"), [64, TT], BF16))
                pit = _s2.enter_context(nc.sbuf_tensor(co.u("z_pi"), [32, TT], I32))
                prow = _s2.enter_context(nc.sbuf_tensor(co.u("z_pr"), [1, 4, TT], F32))
                pcs = _s2.enter_context(nc.sbuf_tensor(co.u("z_pc"), [64, TT], F32))
                pacc = _s2.enter_context(nc.sbuf_tensor(co.u("z_pa"), [64, TT], F32))
                wt = [co.wq.put(wv[:, kc, :], cfg["wv"][kc]) for kc in range(DC)]
                wt += [co.wq.put(wch[:, kc, :], cfg["wch"][kc]) for kc in range(DC)]
                wt += [co.wq.put(wsh[:, kc, :], cfg["wsh"][kc]) for kc in range(DC)]
                wt.append(co.wq.put(omT[:, :], cfg["omT"]))
                wt.append(co.wq.put(wcm[:, :, :], cfg["wcm"].rearrange("m k n -> k m n")))
                bt = [co.ld.put(bsh[:, :], cfg["bsh"]), co.ld.put(rfb[:, :], cfg["rfb"]),
                      co.ld.put(bvb[0:1, :], cfg["bv_row"]),
                      co.ld.put(lt1[0:1, 6:7], cfg["bch"])]
                for tok in bt:
                    co.ld.wait(v, tok)
                    co.ld.wait(s, tok)
                co.ld.wait(t, bt[2])
                co.dv.wait(t)
                co.ev.wait(t)
                t.matmul(ps8[:, 6, :], C["onesrow_f"], bvb[0:1, 0:TT],
                         start=True, stop=True)
                mk = co.mm.inc(t.matmul(ps8[:, 7, :], C["onesrow_f"],
                                        bvb[0:1, TT:2 * TT], start=True, stop=True))
                v.wait_ge(co.mm.h, mk)
                co.vop(v.tensor_copy, bvb[:, 0:TT], ps8[:, 6, :])
                co.vop(v.tensor_copy, bvb[:, TT:2 * TT], ps8[:, 7, :])
                for tok in wt:
                    co.wq.wait(t, tok)
                for tt in range(NTT):
                    sl = slice(tt * TT, (tt + 1) * TT)
                    lt_ = co.ld.put(xts[:, :, :], xap(Xin, si, tt))
                    co.ld.wait(s, lt_)
                    co.ld.wait(v, lt_)
                    emit_ln_tile(co, C, xts, xnt, ps8[0:1, 0:2, :],
                                 ps8[:, 2:4, :], xb, sq, rows, TT)
                    co.dv.wait(t)
                    for tc in range(TT // 128):
                        tcg = tt * (TT // 128) + tc
                        co.dv.wait(t)
                        for kc in range(DC):
                            for vc in range(2):
                                ins = t.matmul(
                                    ps8[:, 4 + vc, :],
                                    xnt[:, kc, tc * 128:(tc + 1) * 128],
                                    wv[:, kc, vc * TT:(vc + 1) * TT],
                                    start=(kc == 0), stop=(kc == DC - 1))
                        mk = co.mm.inc(ins)
                        v.wait_ge(co.mm.h, mk)
                        for vc in range(2):
                            co.vop(v.tensor_tensor, Vt[:, tcg, vc * TT:(vc + 1) * TT],
                                   ps8[:, 4 + vc, :], bvb[:, vc * TT:(vc + 1) * TT],
                                   ALU.add)
                    co.dv.wait(t)
                    for kc in range(DC):
                        ins = t.matmul(ps8[0:1, 6, :], wch[:, kc, :], xnt[:, kc, :],
                                       start=(kc == 0), stop=(kc == DC - 1))
                    mkq = co.mm.inc(ins)
                    s.wait_ge(co.mm.h, mkq)
                    co.sop(s.activation, prow[:, 0, :], ps8[0:1, 6, :], AF.Tanh,
                           bias=lt1[0:1, 6:7])
                    for kc in range(DC):
                        ins = t.matmul(ps8[0:64, 7, :], wsh[:, kc, :], xnt[:, kc, :],
                                       start=(kc == 0), stop=(kc == DC - 1))
                    mksh = co.mm.inc(ins)
                    v.wait_ge(co.mm.h, mksh)
                    co.vop(v.tensor_scalar_add, shl[:, :], ps8[0:64, 7, :],
                           bsh[:, 0:1])
                    d_ = co.dv.count
                    s.wait_ge(co.dv.h, d_)
                    co.sop(s.square, ssq[:, :], shl[:, :])
                    co.ev.wait(t)
                    ins = t.matmul(ps8[0:1, 6, :], C["ones64_f"],
                                   ssq[:, :], start=True, stop=True)
                    mkn = co.mm.inc(ins)
                    s.wait_ge(co.mm.h, mkn)
                    co.sop(s.activation, prow[:, 1, :], ps8[0:1, 6, :], AF.Sqrt)
                    v.wait_ge(co.ev.h, co.ev.count)
                    co.vop(v.tensor_single_scalar, prow[:, 1, :], prow[:, 1, :],
                           1e-12, ALU.max)
                    co.vop(v.reciprocal, prow[:, 1, :], prow[:, 1, :])
                    d_ = co.dv.count
                    t.wait_ge(co.dv.h, d_)
                    ins = t.matmul(ps8[0:64, 7, :], C["onesrow_f"][0:1, 0:64],
                                   prow[:, 1, :], start=True, stop=True)
                    mkb = co.mm.inc(ins)
                    v.wait_ge(co.mm.h, mkb)
                    co.vop(v.tensor_mul, shn[:, :], shl[:, :], ps8[0:64, 7, :])
                    d_ = co.dv.count
                    t.wait_ge(co.dv.h, d_)
                    ins = t.matmul(ps8[0:32, 6, :], omT[:, :], shn[:, :],
                                   start=True, stop=True)
                    mkp = co.mm.inc(ins)
                    v.wait_ge(co.mm.h, mkp)
                    co.vop(v.tensor_scalar_add, prj[:, 0, :], ps8[0:32, 6, :],
                           rfb[:, 0:1])
                    for part, shift in ((1, 16.5), (0, 16.75)):
                        co.vop(v.tensor_scalar, prj[:, 1, :], prj[:, 0, :],
                               1.0 / (2 * math.pi), shift, ALU.mult, ALU.add)
                        co.vop(v.tensor_copy, pit[:, :], prj[:, 1, :])
                        co.vop(v.scalar_tensor_tensor, prj[:, 1, :], pit[:, :],
                               0.5, prj[:, 1, :], ALU.add, ALU.subtract)
                        d2 = co.dv.count
                        s.wait_ge(co.dv.h, d2)
                        lo = 32 * part
                        co.sop(s.activation, phs[32 - lo:64 - lo, :],
                               prj[:, 1, :], AF.Sin,
                               scale=C["cm2pi"][32 - lo:64 - lo, 0:1])
                        co.ev.wait(v)
                    co.ev.wait(v)
                    co.vop(v.tensor_copy, prow[:, 2, :], prow[:, 0, :])
                    for m in range(5):
                        co.ev.wait(t)
                        co.dv.wait(t)
                        ins = t.matmul(ps8[0:64, 6, :], wcm[:, m, :], phs[:, :],
                                       start=True, stop=True)
                        mkm = co.mm.inc(ins)
                        if m == 0:
                            v.wait_ge(co.mm.h, mkm)
                            co.vop(v.tensor_scalar_mul, pacc[:, :],
                                   ps8[0:64, 6, :], cfg["mags"][0])
                        else:
                            co.vop(v.tensor_scalar_mul, prow[:, 3, :],
                                   prow[:, 2, :], cfg["mags"][m])
                            d4 = co.dv.count
                            t.wait_ge(co.dv.h, d4)
                            ins2 = t.matmul(ps8[0:64, 7, :],
                                            C["onesrow_f"][0:1, 0:64],
                                            prow[:, 3, :], start=True, stop=True)
                            mkb2 = co.mm.inc(ins2)
                            v.wait_ge(co.mm.h, max(mkm, mkb2))
                            co.vop(v.tensor_copy, pcs[:, :], ps8[0:64, 7, :])
                            co.vop(v.tensor_mul, pcs[:, :], pcs[:, :],
                                   ps8[0:64, 6, :])
                            co.vop(v.tensor_add, pacc[:, :], pacc[:, :], pcs[:, :])
                            if m < 4:
                                co.vop(v.tensor_mul, prow[:, 2, :], prow[:, 2, :],
                                       prow[:, 0, :])
                    co.vop(v.tensor_copy, PHI[:, sl], pacc[:, :])
                    co.ev.wait(y)
                    co.dv.wait(y)
            co.join()
            scf_mark = 0
            for qt in range(NQT):
                nkc = qt + 1
                W = nkc * 128
                co.dv.wait(y)
                ttok = co.ld.put(Tt[:, 0:W], cfg["Tq"][qt, :, 0:W])
                if scf_mark:
                    t.wait_ge(co.dv.h, scf_mark)
                for kg in range((W + TT - 1) // TT):
                    w0 = kg * TT
                    w1 = min(W, w0 + TT)
                    ins = t.matmul(ps8[:, kg, 0:w1 - w0],
                                   PHI[:, qt * 128:(qt + 1) * 128],
                                   PHI[:, w0:w1], start=True, stop=True)
                mks = co.mm.inc(ins)
                v.wait_ge(co.mm.h, mks)
                co.ld.wait(v, ttok)
                for kg in range((W + TT - 1) // TT):
                    w0 = kg * TT
                    w1 = min(W, w0 + TT)
                    scf_mark = co.vop(v.tensor_mul, scf[:, w0:w1],
                                      ps8[:, kg, 0:w1 - w0], Tt[:, w0:w1])
                co.vop(v.reduce_max, rwb[:, 0:1], scf[:, 0:W], AX.X, negate=True)
                co.vop(v.tensor_mul, rwb[:, 1:2], rwb[:, 0:1], ivt[:, :])
                d_ = co.dv.count
                s.wait_ge(co.dv.h, d_)
                co.sop(s.activation, Pb[:, 0:W], scf[:, 0:W], AF.Exp,
                       bias=rwb[:, 1:2], scale=ivt[:, 0:1],
                       accum_out=rwb[:, 2:3])
                co.ev.wait(v)
                dg = Pb[:, qt * 128:(qt + 1) * 128]
                co.vop(v.reduce_sum, rwb[:, 3:4], dg, AX.X)
                co.vop(v.tensor_tensor, dg, dg, C["tril_b"], ALU.mult)
                co.vop(v.reduce_sum, rwb[:, 4:5], dg, AX.X)
                co.vop(v.tensor_sub, rwb[:, 2:3], rwb[:, 2:3], rwb[:, 3:4])
                co.vop(v.tensor_add, rwb[:, 2:3], rwb[:, 2:3], rwb[:, 4:5])
                co.vop(v.reciprocal, rwb[:, 6:7], rwb[:, 2:3])
                nrm_mark = co.vop(v.tensor_scalar_mul, Pb[:, 0:W], Pb[:, 0:W],
                                  rwb[:, 6:7])
                t.wait_ge(co.dv.h, nrm_mark)
                cp_marks = {}
                for kc in range(nkc):
                    bk = 4 if kc % 2 == 0 else 6
                    view = ps8[:, bk, :].bitcast(BF16)[:, 0:128]
                    if kc >= 2:
                        t.wait_ge(co.dv.h, cp_marks[kc - 2])
                    ins = t.transpose(view,
                                      Pb[:, kc * 128:(kc + 1) * 128], C["ident_b"])
                    mkt = co.mm.inc(ins)
                    v.wait_ge(co.mm.h, mkt)
                    cp_marks[kc] = co.vop(v.tensor_copy, PT[:, kc, :], view)
                t.wait_ge(co.dv.h, co.dv.count)
                for dc in range(DC):
                    dslot = dc % 4
                    co.ev.wait(t)
                    for kc in range(nkc):
                        ins = t.matmul(ps8[:, 5, dslot * 128:(dslot + 1) * 128],
                                       Vt[:, kc, dc * 128:(dc + 1) * 128],
                                       PT[:, kc, :], start=(kc == 0),
                                       stop=(kc == nkc - 1))
                    mka = co.mm.inc(ins)
                    s.wait_ge(co.mm.h, mka)
                    co.sop(s.copy, CXs[:, dc, qt * 128:(qt + 1) * 128],
                           ps8[:, 5, dslot * 128:(dslot + 1) * 128])
            co.join()
            with ExitStack() as _s3:
                wo = _s3.enter_context(nc.sbuf_tensor(co.u("z_wo"), [128, DC, DM], BF16))
                bo = _s3.enter_context(nc.sbuf_tensor(co.u("z_bo"), [128, 8], F32))
                xrt = _s3.enter_context(nc.sbuf_tensor(co.u("z_xr"), [128, DC, TT], F32))
                wt = [co.wq.put(wo[:, kc, :], cfg["wo"][kc]) for kc in range(DC)]
                bt = co.ld.put(bo[:, :], cfg["bo"])
                co.ld.wait(v, bt)
                pst = None
                for tt in range(NTT):
                    if pst is not None:
                        co.ld.wait(y, pst)
                    xt_ = co.ld.put(xrt[:, :, :], xap(Xin, si, tt))
                    if tt == 0:
                        for tok in wt:
                            co.wq.wait(t, tok)
                    co.ld.wait(v, xt_)
                    for oc in range(DC):
                        pb = oc % 2
                        co.dv.wait(t)
                        for kc in range(DC):
                            ins = t.matmul(ps8[:, 6 + pb, :],
                                           wo[:, kc, oc * 128:(oc + 1) * 128],
                                           CXs[:, kc, tt * TT:(tt + 1) * TT],
                                           start=(kc == 0), stop=(kc == DC - 1))
                        mk = co.mm.inc(ins)
                        v.wait_ge(co.mm.h, mk)
                        co.vop(v.scalar_tensor_tensor, xrt[:, oc, :],
                               ps8[:, 6 + pb, :], bo[:, oc:oc + 1],
                               xrt[:, oc, :], ALU.add, ALU.add)
                    co.dv.wait(y)
                    pst = co.ld.put(xap(Xout, si, tt), xrt[:, :, :])
            co.join()


def emit_moe(co, C, Xin, Xout, scratch, cfg):
    nc = co.nc
    t, s, v, g, y = co.t, co.s, co.v, co.g, co.y
    kind = cfg["kind"]
    XN, AACC, SSP = scratch["XN"], scratch["AACC"], scratch["SSP"]
    NE = 8

    with ExitStack() as _st:
        MR = _st.enter_context(nc.sbuf_tensor(co.u("mr"), [8, NSEQ * SEQ], F32))
        ps8 = _st.enter_context(nc.psum_tensor(co.u("m_ps"), [128, 8, TT], F32))
        with ExitStack() as _s2:
            xts = _s2.enter_context(nc.sbuf_tensor(co.u("m_xt"), [128, DC, TT], F32))
            xnt = _s2.enter_context(nc.sbuf_tensor(co.u("m_xn"), [128, DC, TT], BF16))
            xb = _s2.enter_context(nc.sbuf_tensor(co.u("m_xb"), [128, DC, TT], BF16))
            sq = _s2.enter_context(nc.sbuf_tensor(co.u("m_sq"), [128, DC, TT], BF16))
            rows = _s2.enter_context(nc.sbuf_tensor(co.u("m_rw"), [1, 3, TT], F32))
            rWs = _s2.enter_context(nc.sbuf_tensor(co.u("m_rW"), [128, DC, 8], BF16))
            gbB = _s2.enter_context(nc.sbuf_tensor(co.u("m_gb"), [128, 8], F32))
            mt = _s2.enter_context(nc.sbuf_tensor(co.u("m_mt"), [128, 12, 8], F32))
            mrow = _s2.enter_context(nc.sbuf_tensor(co.u("m_mr"), [128, 4], F32))
            rt = [co.wq.put(rWs[:, kc, :], cfg["router"][kc]) for kc in range(DC)]
            if kind == "l3":
                gb = co.ld.put(gbB[0:1, :], cfg["gate_b_row"])
                co.ld.wait(t, gb)
                co.dv.wait(t)
                mk = co.mm.inc(t.matmul(ps8[:, 7, 0:8], C["onesrow_f"],
                                        gbB[0:1, :], start=True, stop=True))
                v.wait_ge(co.mm.h, mk)
                co.vop(v.tensor_copy, gbB[:, :], ps8[:, 7, 0:8])
            for tok in rt:
                co.wq.wait(t, tok)
            sttok = None
            for si in range(NSEQ):
                for tt in range(NTT):
                    lt = co.ld.put(xts[:, :, :], xap(Xin, si, tt))
                    co.ld.wait(s, lt)
                    co.ld.wait(v, lt)
                    if sttok is not None:
                        co.ld.wait(v, sttok)
                    dtok = emit_ln_tile(co, C, xts, xnt, ps8[0:1, 0:2, :],
                                        ps8[:, 2:4, :], xb, sq, rows, TT)
                    y.wait_ge(co.dv.h, dtok)
                    sttok = co.ld.put(xap(XN, si, tt), xnt[:, :, :])
                    for tc in range(TT // 128):
                        gtok = (si * SEQ + tt * TT + tc * 128)
                        co.dv.wait(t)
                        for kc in range(DC):
                            ins = t.matmul(ps8[:, 5, 0:8],
                                           xnt[:, kc, tc * 128:(tc + 1) * 128],
                                           rWs[:, kc, :],
                                           start=(kc == 0), stop=(kc == DC - 1))
                        mk = co.mm.inc(ins)
                        v.wait_ge(co.mm.h, mk)
                        lg = mt[:, 0, :]
                        if kind == "l3":
                            co.vop(v.tensor_tensor, lg, ps8[:, 5, 0:8], gbB[:, :],
                                   ALU.add)
                        else:
                            co.vop(v.tensor_copy, lg, ps8[:, 5, 0:8])
                        co.vop(v.max, mt[:, 2, :], lg)
                        co.vop(v.memset, mt[:, 3, :], -1e30)
                        co.vop(v.tensor_copy, mt[:, 3, 0:1], mt[:, 2, 0:1])
                        co.vop(v.match_replace, mt[:, 4, :], mt[:, 3, :], lg, -1e30)
                        if kind == "l2":
                            co.vop(v.tensor_tensor, mt[:, 6, :], lg, mt[:, 4, :],
                                   ALU.not_equal)
                            co.vop(v.tensor_scalar_mul, mt[:, 1, :], mt[:, 6, :], 0.5)
                        else:
                            co.vop(v.tensor_copy, mt[:, 3, 1:2], mt[:, 2, 1:2])
                            co.vop(v.match_replace, mt[:, 5, :], mt[:, 3, :], lg,
                                   -1e30)
                            co.vop(v.tensor_tensor, mt[:, 6, :], lg, mt[:, 4, :],
                                   ALU.not_equal)
                            co.vop(v.tensor_tensor, mt[:, 7, :], lg, mt[:, 5, :],
                                   ALU.not_equal)
                            co.vop(v.tensor_sub, mt[:, 7, :], mt[:, 7, :],
                                   mt[:, 6, :])
                            co.vop(v.tensor_sub, mrow[:, 0:1], mt[:, 2, 0:1],
                                   mt[:, 2, 1:2])
                            d0 = co.dv.count
                            s.wait_ge(co.dv.h, d0)
                            co.sop(s.activation, mrow[:, 0:1], mrow[:, 0:1],
                                   AF.Sigmoid)
                            v.wait_ge(co.ev.h, co.ev.count)
                            co.vop(v.tensor_scalar, mrow[:, 2:3], mrow[:, 0:1],
                                   -1.0, 1.0, ALU.mult, ALU.add)
                            co.vop(v.tensor_scalar, mt[:, 6, :], mt[:, 6, :],
                                   mrow[:, 0:1], 0.5, ALU.mult, ALU.mult)
                            co.vop(v.tensor_scalar, mt[:, 7, :], mt[:, 7, :],
                                   mrow[:, 2:3], 0.5, ALU.mult, ALU.mult)
                            co.vop(v.tensor_add, mt[:, 1, :], mt[:, 6, :],
                                   mt[:, 7, :])
                        d1 = co.dv.count
                        t.wait_ge(co.dv.h, d1)
                        ins = t.transpose(ps8[0:8, 6, 0:128], mt[:, 1, :],
                                          C["ident_f"])
                        mk2 = co.mm.inc(ins)
                        v.wait_ge(co.mm.h, mk2)
                        co.vop(v.tensor_copy, MR[:, gtok:gtok + 128],
                               ps8[0:8, 6, 0:128])
        co.join()
        a_store = {}
        for e in range(NE):
            if kind == "l2":
                _emit_moe_l2_expert(co, C, Xin, Xout, XN, AACC, MR, ps8, cfg,
                                    e, NE, a_store)
            else:
                _emit_moe_l3_expert(co, C, Xin, Xout, XN, AACC, SSP, MR, ps8,
                                    cfg, e, NE, a_store)
            co.join()


def _emit_moe_l2_expert(co, C, Xin, Xout, XN, AACC, MR, ps8, cfg, e, NE, a_store):
    nc = co.nc
    t, s, v, y = co.t, co.s, co.v, co.y
    HC = 22
    with ExitStack() as _st:
        gW = _st.enter_context(nc.sbuf_tensor(co.u("e_gw"), [128, DC, 2816], BF16))
        uW = _st.enter_context(nc.sbuf_tensor(co.u("e_uw"), [128, DC, 2816], BF16))
        dW = _st.enter_context(nc.sbuf_tensor(co.u("e_dw"), [128, HC, DM], BF16))
        xnt = _st.enter_context(nc.sbuf_tensor(co.u("e_xn"), [128, DC, TT], BF16))
        hbt = _st.enter_context(nc.sbuf_tensor(co.u("e_hb"), [128, HC, TT], BF16))
        at = _st.enter_context(nc.sbuf_tensor(co.u("e_at"), [128, DC, TT], F32))
        gtmp = _st.enter_context(nc.sbuf_tensor(co.u("e_gt"), [128, TT], BF16))
        mbs = _st.enter_context(nc.sbuf_tensor(co.u("e_mb"), [128, TT], F32))
        ytm = _st.enter_context(nc.sbuf_tensor(co.u("e_yt"), [128, TT], F32))
        wt = [co.wq.put(gW[:, kc, :], cfg["gateW"][e, kc]) for kc in range(DC)]
        wt += [co.wq.put(uW[:, kc, :], cfg["upW"][e, kc]) for kc in range(DC)]
        wt += [co.wq.put(dW[:, hc, :], cfg["downW"][e, hc]) for hc in range(HC)]
        for tok in wt:
            co.wq.wait(t, tok)
        psh = ps8[:, 0:4, :]
        psy = ps8[:, 4:6, :]
        psm = ps8[:, 6, :]
        evm, dvm, dvy = {}, {}, {}
        mb_mark = 0
        for si in range(NSEQ):
            for tt in range(NTT):
                co.mm.wait(y)
                xtok = co.ld.put(xnt[:, :, :], xap(XN, si, tt))
                co.ld.wait(t, xtok)
                for hc in range(HC):
                    pb = (hc % 2) * 2
                    if evm.get(pb):
                        t.wait_ge(co.ev.h, evm[pb])
                    if dvm.get(pb):
                        t.wait_ge(co.dv.h, dvm[pb])
                    for kc in range(DC):
                        t.matmul(psh[:, pb, :], gW[:, kc, hc * 128:(hc + 1) * 128],
                                 xnt[:, kc, :], start=(kc == 0), stop=(kc == DC - 1))
                    for kc in range(DC):
                        ins = t.matmul(psh[:, pb + 1, :],
                                       uW[:, kc, hc * 128:(hc + 1) * 128],
                                       xnt[:, kc, :], start=(kc == 0),
                                       stop=(kc == DC - 1))
                    mmk = co.mm.inc(ins)
                    s.wait_ge(co.mm.h, mmk)
                    if dvm.get("g"):
                        s.wait_ge(co.dv.h, dvm["g"])
                    evm[pb] = co.sop(s.activation, gtmp[:, :], psh[:, pb, :], AF.Silu)
                    v.wait_ge(co.ev.h, evm[pb])
                    dvm[pb] = dvm["g"] = co.vop(v.tensor_mul, hbt[:, hc, :],
                                                gtmp[:, :], psh[:, pb + 1, :])
                gtok = si * SEQ + tt * TT
                if mb_mark:
                    t.wait_ge(co.dv.h, mb_mark)
                ins = t.matmul(psm, C["sel8"][:, e * 128:(e + 1) * 128],
                               MR[:, gtok:gtok + TT], start=True, stop=True)
                mkb = co.mm.inc(ins)
                prev = a_store.get((si, tt))
                if prev is not None:
                    co.ld.wait(y, prev)
                if a_store.get("last") is not None:
                    co.ld.wait(y, a_store["last"])
                at_tok = co.ld.put(at[:, :, :],
                                   xap(Xin if e == 0 else AACC, si, tt))
                co.ld.wait(v, at_tok)
                v.wait_ge(co.mm.h, mkb)
                mb_mark = co.vop(v.tensor_copy, mbs[:, :], psm)
                t.wait_ge(co.dv.h, dvm["g"])
                for dc in range(DC):
                    pb = dc % 2
                    if dvy.get(pb):
                        t.wait_ge(co.dv.h, dvy[pb])
                    for hc in range(HC):
                        ins = t.matmul(psy[:, pb, :],
                                       dW[:, hc, dc * 128:(dc + 1) * 128],
                                       hbt[:, hc, :], start=(hc == 0),
                                       stop=(hc == HC - 1))
                    mmk = co.mm.inc(ins)
                    v.wait_ge(co.mm.h, mmk)
                    co.vop(v.tensor_mul, ytm[:, :], psy[:, pb, :], mbs[:, :])
                    dvy[pb] = co.vop(v.tensor_tensor, at[:, dc, :], at[:, dc, :],
                                     ytm[:, :], ALU.add)
                co.dv.wait(y)
                a_store[(si, tt)] = a_store["last"] = co.ld.put(
                    xap(Xout if e == NE - 1 else AACC, si, tt), at[:, :, :])


def _emit_moe_l3_expert(co, C, Xin, Xout, XN, AACC, SSP, MR, ps8, cfg, e, NE,
                        a_store):
    nc = co.nc
    t, s, v, y = co.t, co.s, co.v, co.y
    FC = 16
    with ExitStack() as _st:
        w1 = _st.enter_context(nc.sbuf_tensor(co.u("e_w1"), [128, DC, 2048], BF16))
        w2 = _st.enter_context(nc.sbuf_tensor(co.u("e_w2"), [128, DC, 2048], BF16))
        wg = _st.enter_context(nc.sbuf_tensor(co.u("e_wg"), [128, FC, 2048], BF16))
        bb = _st.enter_context(nc.sbuf_tensor(co.u("e_bb"), [128, 3, FC], F32))
        xnt2 = _st.enter_context(nc.sbuf_tensor(co.u("e_xn"), [128, 2, DC, TT], BF16))
        x1s = _st.enter_context(nc.sbuf_tensor(co.u("e_x1"), [128, TT], F32))
        zb = _st.enter_context(nc.sbuf_tensor(co.u("e_zb"), [128, FC, TT], BF16))
        sb = _st.enter_context(nc.sbuf_tensor(co.u("e_sb"), [128, FC, TT], BF16))
        wt = [co.wq.put(w1[:, kc, :], cfg["w1"][e, kc]) for kc in range(DC)]
        wt += [co.wq.put(w2[:, kc, :], cfg["w2"][e, kc]) for kc in range(DC)]
        wt += [co.wq.put(wg[:, fc, :], cfg["wg"][e, fc]) for fc in range(FC)]
        bt = [co.ld.put(bb[:, 0, :], cfg["b1"][e]),
              co.ld.put(bb[:, 1, :], cfg["b2"][e]),
              co.ld.put(bb[:, 2, :], cfg["bg"][e])]
        for tok in wt:
            co.wq.wait(t, tok)
        for tok in bt:
            co.ld.wait(v, tok)
            co.ld.wait(s, tok)
        ps4 = ps8[:, 0:4, :]
        pss = ps8[:, 4:6, :]
        st_tok = None
        dvm, evs = {}, {}
        tiles = [(si, tt) for si in range(NSEQ) for tt in range(NTT)]
        xtoks = {}
        h_marks = {}
        xtoks[0] = co.ld.put(xnt2[:, 0, :, :], xap(XN, *tiles[0]))
        for ti, (si, tt) in enumerate(tiles):
            if True:
                par = ti % 2
                xnt = xnt2[:, par, :, :]
                if ti + 1 < len(tiles):
                    if ti >= 1:
                        y.wait_ge(co.mm.h, h_marks[ti - 1])
                    xtoks[ti + 1] = co.ld.put(xnt2[:, (ti + 1) % 2, :, :],
                                              xap(XN, *tiles[ti + 1]))
                co.ld.wait(t, xtoks[ti])
                for fc in range(FC):
                    pb = (fc % 2) * 2
                    if dvm.get(pb):
                        t.wait_ge(co.dv.h, dvm[pb])
                    for kc in range(DC):
                        t.matmul(ps4[:, pb, :], w1[:, kc, fc * 128:(fc + 1) * 128],
                                 xnt[:, kc, :], start=(kc == 0), stop=(kc == DC - 1))
                    for kc in range(DC):
                        ins = t.matmul(ps4[:, pb + 1, :],
                                       w2[:, kc, fc * 128:(fc + 1) * 128],
                                       xnt[:, kc, :], start=(kc == 0),
                                       stop=(kc == DC - 1))
                    mmk = co.mm.inc(ins)
                    v.wait_ge(co.mm.h, mmk)
                    co.vop(v.tensor_scalar_add, x1s[:, :], ps4[:, pb, :],
                           bb[:, 0, fc:fc + 1])
                    dvm[pb] = co.vop(v.scalar_tensor_tensor, zb[:, fc, :],
                                     ps4[:, pb + 1, :], bb[:, 1, fc:fc + 1],
                                     x1s[:, :], ALU.add, ALU.mult)
                h_marks[ti] = co.mm.count
                if st_tok is not None:
                    co.ld.wait(s, st_tok)
                for sc in range(FC):
                    pb = sc % 2
                    if evs.get(pb):
                        t.wait_ge(co.ev.h, evs[pb])
                    if sc == 0:
                        t.wait_ge(co.dv.h, max(dvm[0], dvm[2]))
                    for fc in range(FC):
                        ins = t.matmul(pss[:, pb, :],
                                       wg[:, fc, sc * 128:(sc + 1) * 128],
                                       zb[:, fc, :], start=(fc == 0),
                                       stop=(fc == FC - 1))
                    mmk = co.mm.inc(ins)
                    s.wait_ge(co.mm.h, mmk)
                    evs[pb] = co.sop(s.activation, sb[:, sc, :], pss[:, pb, :],
                                     AF.Silu, bias=bb[:, 2, sc:sc + 1])
                y.wait_ge(co.ev.h, co.ev.count)
                st_tok = co.ld.put(
                    SSP[si, :, :, tt * TT:(tt + 1) * TT].rearrange("c p n -> p c n"),
                    sb[:, :, :])
        co.join()
    with ExitStack() as _st:
        w3 = _st.enter_context(nc.sbuf_tensor(co.u("e_w3"), [128, FC, DM], BF16))
        b3 = _st.enter_context(nc.sbuf_tensor(co.u("e_b3"), [128, 8], F32))
        sbt2 = _st.enter_context(nc.sbuf_tensor(co.u("e_st"), [128, 2, FC, TT], BF16))
        at = _st.enter_context(nc.sbuf_tensor(co.u("e_at"), [128, DC, TT], F32))
        mbs = _st.enter_context(nc.sbuf_tensor(co.u("e_mb"), [128, TT], F32))
        ytm = _st.enter_context(nc.sbuf_tensor(co.u("e_yt"), [128, TT], F32))
        wt = [co.wq.put(w3[:, fc, :], cfg["w3"][e, fc]) for fc in range(FC)]
        bt = co.ld.put(b3[:, :], cfg["b3"][e])
        for tok in wt:
            co.wq.wait(t, tok)
        co.ld.wait(v, bt)
        psy = ps8[:, 0:2, :]
        psm = ps8[:, 6, :]
        dvy = {}
        mb_mark = 0
        tiles = [(si, tt) for si in range(NSEQ) for tt in range(NTT)]
        stoks = {}
        y_marks = {}
        stoks[0] = co.ld.put(
            sbt2[:, 0, :, :],
            SSP[tiles[0][0], :, :, tiles[0][1] * TT:(tiles[0][1] + 1) * TT]
            .rearrange("c p n -> p c n"))
        for ti, (si, tt) in enumerate(tiles):
            if True:
                sbt = sbt2[:, ti % 2, :, :]
                if ti + 1 < len(tiles):
                    if ti >= 1:
                        y.wait_ge(co.mm.h, y_marks[ti - 1])
                    nsi, ntt = tiles[ti + 1]
                    stoks[ti + 1] = co.ld.put(
                        sbt2[:, (ti + 1) % 2, :, :],
                        SSP[nsi, :, :, ntt * TT:(ntt + 1) * TT]
                        .rearrange("c p n -> p c n"))
                co.ld.wait(t, stoks[ti])
                gtok = si * SEQ + tt * TT
                if mb_mark:
                    t.wait_ge(co.dv.h, mb_mark)
                ins = t.matmul(psm, C["sel8"][:, e * 128:(e + 1) * 128],
                               MR[:, gtok:gtok + TT], start=True, stop=True)
                mkb = co.mm.inc(ins)
                prev = a_store.get((si, tt))
                if prev is not None:
                    co.ld.wait(y, prev)
                if a_store.get("last") is not None:
                    co.ld.wait(y, a_store["last"])
                at_tok = co.ld.put(at[:, :, :],
                                   xap(Xin if e == 0 else AACC, si, tt))
                co.ld.wait(v, at_tok)
                v.wait_ge(co.mm.h, mkb)
                mb_mark = co.vop(v.tensor_copy, mbs[:, :], psm)
                for dc in range(DC):
                    pb = dc % 2
                    if dvy.get(pb):
                        t.wait_ge(co.dv.h, dvy[pb])
                    for fc in range(FC):
                        ins = t.matmul(psy[:, pb, :],
                                       w3[:, fc, dc * 128:(dc + 1) * 128],
                                       sbt[:, fc, :], start=(fc == 0),
                                       stop=(fc == FC - 1))
                    mmk = co.mm.inc(ins)
                    v.wait_ge(co.mm.h, mmk)
                    co.vop(v.scalar_tensor_tensor, ytm[:, :], psy[:, pb, :],
                           b3[:, dc:dc + 1], mbs[:, :], ALU.add, ALU.mult)
                    dvy[pb] = co.vop(v.tensor_tensor, at[:, dc, :], at[:, dc, :],
                                     ytm[:, :], ALU.add)
                y_marks[ti] = co.mm.count
                co.dv.wait(y)
                a_store[(si, tt)] = a_store["last"] = co.ld.put(
                    xap(Xout if e == NE - 1 else AACC, si, tt), at[:, :, :])


NH = 16
HD = 64
NQT = SEQ // 128


def emit_attn(co, C, Xin, Xout, scratch, cfg):
    nc = co.nc
    t, s, v, g, y = co.t, co.s, co.v, co.g, co.y
    kind = cfg["kind"]
    causal = cfg["causal"]
    QSP, CTX = scratch["QSP"], scratch["CTX"]

    with (
        nc.sbuf_tensor(co.u("Ks"), [128, DC, SEQ], BF16) as Ks,
        nc.sbuf_tensor(co.u("Vt"), [128, NQT, DM], BF16) as Vt,
        nc.sbuf_tensor(co.u("Pb"), [128, SEQ], BF16) as Pb,
        nc.sbuf_tensor(co.u("PTb"), [128, NQT, 512], BF16) as PT,
        nc.sbuf_tensor(co.u("rwb"), [128, 8], F32) as rwb,
        nc.sbuf_tensor(co.u("cxs"), [128, TT], BF16) as cxs,
        nc.psum_tensor(co.u("ps8"), [128, 8, TT], F32) as ps8,
    ):
        ps_sc = ps8[:, 0:4, :].rearrange("p a n -> p (a n)")
        ps_ptb = ps8[:, 4, :].bitcast(BF16)
        ps_cx = ps8[0:64, 5, :]
        ps_pj = ps8[:, 6:8, :]
        ps_rp = ps8[:, 5, :]
        ps_lst = ps8[0:1, 0:2, :]
        ps_lbc = ps8[:, 2:4, :]
        for si in range(NSEQ):
            with ExitStack() as _st:
                xts = _st.enter_context(nc.sbuf_tensor(co.u("a_xts"), [128, DC, TT], F32))
                xnt = _st.enter_context(nc.sbuf_tensor(co.u("a_xnt"), [128, DC, TT], BF16))
                xb = _st.enter_context(nc.sbuf_tensor(co.u("a_xb"), [128, DC, TT], BF16))
                sq = _st.enter_context(nc.sbuf_tensor(co.u("a_sq"), [128, DC, TT], BF16))
                rows = _st.enter_context(nc.sbuf_tensor(co.u("a_rws"), [1, 3, TT], F32))
                Qtile = _st.enter_context(nc.sbuf_tensor(co.u("a_qt"), [128, DC, TT], BF16))
                qraw = _st.enter_context(nc.sbuf_tensor(co.u("a_qr"), [128, TT], BF16))
                tr1 = _st.enter_context(nc.sbuf_tensor(co.u("a_t1"), [128, TT], BF16))
                wqkv = _st.enter_context(nc.sbuf_tensor(co.u("a_wq"), [128, DC, 3 * DM], BF16))
                bs = _st.enter_context(nc.sbuf_tensor(co.u("a_bs"), [128, 24], F32))
                rc2 = _st.enter_context(nc.sbuf_tensor(co.u("a_rc"), [128, 2, TT], F32))
                rc3 = _st.enter_context(nc.sbuf_tensor(co.u("a_r3"), [128, 2, DC], F32))
                bvb = _st.enter_context(nc.sbuf_tensor(co.u("a_bv"), [128, DM], F32))
                wt = []
                if kind == "l3":
                    for kc in range(DC):
                        wt.append(co.wq.put(wqkv[:, kc, :], cfg["wqkv"][kc]))
                else:
                    for nm, off in (("wq", 0), ("wk", DM), ("wv", 2 * DM)):
                        for kc in range(DC):
                            wt.append(co.wq.put(wqkv[:, kc, off:off + DM],
                                                cfg[nm][kc]))
                bt = []
                if kind == "l2":
                    bt.append(co.ld.put(bs[:, 0:8], cfg["bq"]))
                    bt.append(co.ld.put(bs[:, 8:16], cfg["bk"]))
                if kind == "l3":
                    bt.append(co.ld.put(bs[:, 0:24], cfg["bqkv"]))
                    bt.append(co.ld.put(rc3[:, 0, :], cfg["ropeC"]))
                    bt.append(co.ld.put(rc3[:, 1, :], cfg["ropeS"]))
                if kind in ("l2", "l3"):
                    bvt = co.ld.put(bvb[0:1, :], cfg["bv_row"])
                    co.ld.wait(t, bvt)
                    co.dv.wait(t)
                    co.ev.wait(t)
                    t.matmul(ps_pj[:, 0, :], C["onesrow_f"], bvb[0:1, 0:TT],
                             start=True, stop=True)
                    mk = co.mm.inc(t.matmul(ps_pj[:, 1, :], C["onesrow_f"],
                                            bvb[0:1, TT:2 * TT], start=True, stop=True))
                    v.wait_ge(co.mm.h, mk)
                    co.vop(v.tensor_copy, bvb[:, 0:TT], ps_pj[:, 0, :])
                    co.vop(v.tensor_copy, bvb[:, TT:2 * TT], ps_pj[:, 1, :])
                for tok in wt:
                    co.wq.wait(t, tok)
                for tok in bt:
                    co.ld.wait(s, tok)
                    co.ld.wait(v, tok)
                qst = None
                for tt in range(NTT):
                    lt = co.ld.put(xts[:, :, :], xap(Xin, si, tt))
                    co.ld.wait(s, lt)
                    co.ld.wait(v, lt)
                    emit_ln_tile(co, C, xts, xnt, ps_lst, ps_lbc, xb, sq, rows, TT)
                    if kind == "l2":
                        rt = co.ld.put(rc2[:, :, :],
                                       cfg["ropeCS"][:, :, tt * TT:(tt + 1) * TT]
                                       .rearrange("p a n -> p a n"))
                        co.ld.wait(v, rt)
                    co.dv.wait(t)
                    for which in (0, 1):
                        for oc in range(DC):
                            pb = oc % 2
                            co.ev.wait(t)
                            co.dv.wait(t)
                            for kc in range(DC):
                                ins = t.matmul(
                                    ps_pj[:, pb, :],
                                    wqkv[:, kc, which * DM + oc * 128:
                                         which * DM + (oc + 1) * 128],
                                    xnt[:, kc, :],
                                    start=(kc == 0), stop=(kc == DC - 1))
                            mk = co.mm.inc(ins)
                            dst = (Qtile[:, oc, :] if which == 0
                                   else Ks[:, oc, tt * TT:(tt + 1) * TT])
                            qsc = 0.125 if which == 0 else 1.0
                            if kind == "l1":
                                v.wait_ge(co.mm.h, mk)
                                co.vop(v.tensor_scalar_mul, dst,
                                       ps_pj[:, pb, :], qsc)
                                continue
                            bias = bs[:, which * 8 + oc:which * 8 + oc + 1]
                            v.wait_ge(co.mm.h, mk)
                            co.vop(v.tensor_scalar, qraw[:, :],
                                   ps_pj[:, pb, :], bias, qsc, ALU.add, ALU.mult)
                            co.dv.wait(t)
                            ins = t.matmul(ps_rp, C["prm_b"], qraw[:, :],
                                           start=True, stop=True)
                            mk2 = co.mm.inc(ins)
                            if kind == "l2":
                                cc, ss_ = rc2[:, 0, :], rc2[:, 1, :]

                                co.ev.wait(v)
                                co.vop(v.tensor_mul, tr1[:, :], qraw[:, :], cc)
                                v.wait_ge(co.mm.h, mk2)
                                co.vop(v.tensor_mul, qraw[:, :], ps_rp, ss_)
                                co.vop(v.tensor_add, dst, tr1[:, :], qraw[:, :])
                            else:
                                co.ev.wait(v)
                                co.vop(v.tensor_scalar_mul, tr1[:, :], qraw[:, :],
                                       rc3[:, 0, oc:oc + 1])
                                v.wait_ge(co.mm.h, mk2)
                                co.vop(v.tensor_scalar_mul, qraw[:, :], ps_rp,
                                       rc3[:, 1, oc:oc + 1])
                                co.vop(v.tensor_add, dst, tr1[:, :], qraw[:, :])
                    co.dv.wait(y)
                    co.ev.wait(y)
                    qst = co.ld.put(
                        QSP[:, :, tt * TT:(tt + 1) * TT].rearrange("c p n -> p c n"),
                        Qtile[:, :, :])
                    for tc in range(TT // 128):
                        tcg = tt * (TT // 128) + tc
                        co.dv.wait(t)
                        co.ev.wait(t)
                        for kc in range(DC):
                            for vc in range(2):
                                ins = t.matmul(
                                    ps_pj[:, vc, :],
                                    xnt[:, kc, tc * 128:(tc + 1) * 128],
                                    wqkv[:, kc, 2 * DM + vc * TT:2 * DM + (vc + 1) * TT],
                                    start=(kc == 0), stop=(kc == DC - 1))
                        mk = co.mm.inc(ins)
                        v.wait_ge(co.mm.h, mk)
                        for vc in range(2):
                            if kind == "l1":
                                co.vop(v.tensor_copy,
                                       Vt[:, tcg, vc * TT:(vc + 1) * TT],
                                       ps_pj[:, vc, :])
                            else:
                                co.vop(v.tensor_tensor,
                                       Vt[:, tcg, vc * TT:(vc + 1) * TT],
                                       ps_pj[:, vc, :],
                                       bvb[:, vc * TT:(vc + 1) * TT], ALU.add)
                    if qst is not None:
                        co.ld.wait(y, qst)
                    co.ev.wait(y)
                    co.dv.wait(y)
            co.join()
            with ExitStack() as _st:
                Qh = _st.enter_context(nc.sbuf_tensor(co.u("a_qh"), [128, SEQ], BF16))
                st_ = {"exp": {0: 0, 1: 0}, "rmx": {}, "nrm": {}, "ctx_ev": 0,
                       "sc": [ps_sc, ps_sc]}

                def nkc_of(qt):
                    return (qt + 1) if causal else NQT

                def emit_scores(h, qt):
                    hb, hc = (h % 2) * 64, h // 2
                    ngrp = (qt // 4 + 1) if causal else 4
                    psc = st_["sc"][qt % 2]
                    em = max(st_["exp"].values())
                    if em:
                        t.wait_ge(co.ev.h, em)
                    rm = max(st_["rmx"].values()) if st_["rmx"] else 0
                    if rm:
                        t.wait_ge(co.dv.h, rm)
                    for kg in range(ngrp):
                        ins = t.matmul(
                            psc[:, kg * TT:(kg + 1) * TT],
                            Qh[hb:hb + 64, qt * 128:(qt + 1) * 128],
                            Ks[hb:hb + 64, hc, kg * TT:(kg + 1) * TT],
                            start=True, stop=True)
                    st_["mks"] = co.mm.inc(ins)

                def emit_softmax(qt):
                    W = nkc_of(qt) * 128
                    psc = st_["sc"][qt % 2]
                    v.wait_ge(co.mm.h, st_["mks"])
                    st_["rmx"][qt % 2] = co.vop(v.reduce_max, rwb[:, 0:1],
                                                psc[:, 0:W], AX.X, negate=True)
                    s.wait_ge(co.dv.h, st_["rmx"][qt % 2])
                    st_["exp"][qt % 2] = co.sop(s.activation, Pb[:, 0:W],
                                                psc[:, 0:W],
                                                AF.Exp, bias=rwb[:, 0:1],
                                                accum_out=rwb[:, 2:3])
                    co.ev.wait(v)
                    if causal:
                        dg = Pb[:, qt * 128:(qt + 1) * 128]
                        co.vop(v.reduce_sum, rwb[:, 3:4], dg, AX.X)
                        co.vop(v.tensor_tensor, dg, dg, C["tril_b"], ALU.mult)
                        co.vop(v.reduce_sum, rwb[:, 4:5], dg, AX.X)
                        co.vop(v.tensor_sub, rwb[:, 2:3], rwb[:, 2:3], rwb[:, 3:4])
                        co.vop(v.tensor_add, rwb[:, 2:3], rwb[:, 2:3], rwb[:, 4:5])
                    co.vop(v.reciprocal, rwb[:, 6:7], rwb[:, 2:3])
                    st_["nrm"][qt] = co.vop(v.tensor_scalar_mul, Pb[:, 0:W],
                                            Pb[:, 0:W], rwb[:, 6:7])

                def emit_pt_av(h, qt):
                    hb, hc = (h % 2) * 64, h // 2
                    nkc = nkc_of(qt)
                    t.wait_ge(co.dv.h, st_["nrm"].pop(qt))
                    sub = (qt % 4) * 128
                    cp_marks = {}
                    for kc in range(nkc):
                        bk = 4 if kc % 2 == 0 else 6
                        view = ps8[:, bk, :].bitcast(BF16)[:, 0:128]
                        if kc >= 2:
                            sem, mk_ = cp_marks[kc - 2]
                            t.wait_ge(sem.h, mk_)
                        ins = t.transpose(view, Pb[:, kc * 128:(kc + 1) * 128],
                                          C["ident_b"])
                        mkt = co.mm.inc(ins)
                        if kc % 2 == 0:
                            v.wait_ge(co.mm.h, mkt)
                            cp_marks[kc] = (co.dv, co.vop(
                                v.tensor_copy, PT[:, kc, sub:sub + 128], view))
                        else:
                            s.wait_ge(co.mm.h, mkt)
                            cp_marks[kc] = (co.ev, co.sop(
                                s.copy, PT[:, kc, sub:sub + 128], view))
                    gmax = min(((qt // 4) * 4 + 4), NQT) if causal else NQT
                    if causal:
                        for kc in range(nkc, gmax):
                            co.vop(v.memset, PT[:, kc, sub:sub + 128], 0.0)
                    if qt % 4 == 3:
                        d_ = co.dv.count
                        t.wait_ge(co.dv.h, d_)
                        e_ = co.ev.count
                        t.wait_ge(co.ev.h, e_)
                        for kc in range(gmax):
                            ins = t.matmul(ps_cx[:, :],
                                           Vt[:, kc, h * HD:(h + 1) * HD],
                                           PT[:, kc, :],
                                           start=(kc == 0), stop=(kc == gmax - 1))
                        mka = co.mm.inc(ins)
                        s.wait_ge(co.mm.h, mka)
                        st_["ctx_ev"] = co.sop(s.copy, cxs[0:64, :], ps_cx[:, :])
                        y.wait_ge(co.ev.h, st_["ctx_ev"])
                        ct = co.ld.put(
                            CTX[hc, hb:hb + 64, (qt - 3) * 128:(qt + 1) * 128],
                            cxs[0:64, :])
                        co.ld.wait(s, ct)

                prev = None
                for h in range(NH):
                    hb = (h % 2) * 64
                    hc = h // 2
                    y.wait_ge(co.mm.h, co.mm.count)
                    qt_ld = co.ld.put(Qh[hb:hb + 64, :], QSP[hc, hb:hb + 64, :])
                    co.ld.wait(t, qt_ld)
                    for qt in range(NQT):
                        emit_scores(h, qt)
                        if prev is not None:
                            emit_pt_av(*prev)
                        emit_softmax(qt)
                        prev = (h, qt)
                emit_pt_av(*prev)
            co.join()
            with ExitStack() as _st:
                wo = _st.enter_context(nc.sbuf_tensor(co.u("a_wo"), [128, DC, DM], BF16))
                bo = _st.enter_context(nc.sbuf_tensor(co.u("a_bo"), [128, 8], F32))
                cxt = _st.enter_context(nc.sbuf_tensor(co.u("a_cx"), [128, DC, TT], BF16))
                xrt = _st.enter_context(nc.sbuf_tensor(co.u("a_xr"), [128, DC, TT], F32))
                wt = []
                if kind != "l1":
                    wt = [co.wq.put(wo[:, kc, :], cfg["wo"][kc]) for kc in range(DC)]
                    bt = co.ld.put(bo[:, :], cfg["bo"])
                    co.ld.wait(v, bt)
                pst = None
                for tt in range(NTT):
                    if pst is not None:
                        co.ld.wait(y, pst)
                    ct = co.ld.put(cxt[:, :, :],
                                   CTX[:, :, tt * TT:(tt + 1) * TT]
                                   .rearrange("c p n -> p c n"))
                    xt_ = co.ld.put(xrt[:, :, :], xap(Xin, si, tt))
                    if kind == "l1":
                        co.ld.wait(v, ct)
                        co.ld.wait(v, xt_)
                        for dc in range(DC):
                            co.vop(v.tensor_tensor, xrt[:, dc, :], xrt[:, dc, :],
                                   cxt[:, dc, :], ALU.add)
                    else:
                        if tt == 0:
                            for tok in wt:
                                co.wq.wait(t, tok)
                        co.ld.wait(t, ct)
                        co.ld.wait(v, xt_)
                        for oc in range(DC):
                            pb = oc % 2
                            co.dv.wait(t)
                            for kc in range(DC):
                                ins = t.matmul(ps_pj[:, pb, :],
                                               wo[:, kc, oc * 128:(oc + 1) * 128],
                                               cxt[:, kc, :],
                                               start=(kc == 0), stop=(kc == DC - 1))
                            mk = co.mm.inc(ins)
                            v.wait_ge(co.mm.h, mk)
                            co.vop(v.scalar_tensor_tensor, xrt[:, oc, :],
                                   ps_pj[:, pb, :], bo[:, oc:oc + 1],
                                   xrt[:, oc, :], ALU.add, ALU.add)
                    co.dv.wait(y)
                    pst = co.ld.put(xap(Xout, si, tt), xrt[:, :, :])
            co.join()


def _t(a):
    return np.ascontiguousarray(np.asarray(a).T)


def _chunk_k(a, name):
    a = np.ascontiguousarray(a)
    K, M = a.shape
    return a.reshape(K // 128, 128, M)


def _bias_cols(b):
    b = np.asarray(b)
    return np.ascontiguousarray(b.reshape(-1, 128).T)


def make_consts():
    C = {}
    C["ones128_b"] = np.ones((128, 1), dtype=BF)
    C["onesrow_b"] = np.ones((1, 128), dtype=BF)
    C["onesrow_f"] = np.ones((1, 128), dtype=np.float32)
    C["eps1"] = np.full((1, 1), EPS, dtype=np.float32)
    C["c127i"] = np.full((128, 1), 1.0 / 127.0, dtype=np.float32)
    C["ident_b"] = np.eye(128, dtype=BF)
    prm = np.zeros((128, 128), dtype=BF)
    for b in (0, 64):
        for j in range(32):
            prm[b + 32 + j, b + j] = -1.0
            prm[b + j, b + 32 + j] = 1.0
    C["prm_b"] = prm
    C["ident_f"] = np.eye(128, dtype=np.float32)
    C["cm1"] = np.full((128, 1), -1.0, dtype=np.float32)
    C["c1"] = np.full((128, 1), 1.0, dtype=np.float32)
    C["c2pi"] = np.full((128, 1), 2.0 * np.pi, dtype=np.float32)
    C["cm2pi"] = np.full((128, 1), -2.0 * np.pi, dtype=np.float32)
    C["ones64_f"] = np.ones((64, 1), dtype=np.float32)
    C["tril_b"] = np.tril(np.ones((128, 128), dtype=np.float32)).astype(BF)
    C["c0125"] = np.full((128, 1), 0.125, dtype=np.float32)
    C["iota8"] = np.broadcast_to(np.arange(8, dtype=np.float32), (128, 8)).copy()
    sel = np.zeros((8, 8 * 128), dtype=np.float32)
    for e in range(8):
        sel[e, e * 128:(e + 1) * 128] = 1.0
    C["sel8"] = sel
    return C


def declare_consts(nc, C):
    out = {}
    for k, a in C.items():
        dt = BF16 if a.dtype == BF else F32
        out[k] = nc.dram_tensor("c_" + k, list(a.shape), dt, kind="ExternalInput").ap()
    return out


def load_consts(co, dC):
    nc = co.nc
    sb = {}
    toks = []
    specs = {
        "ones128_b": ([128, 1], BF16), "onesrow_b": ([1, 128], BF16),
        "onesrow_f": ([1, 128], F32), "eps1": ([1, 1], F32),
        "c127i": ([128, 1], F32), "ident_b": ([128, 128], BF16),
        "iota8": ([128, 8], F32), "sel8": ([8, 8 * 128], F32),
        "prm_b": ([128, 128], BF16), "c0125": ([128, 1], F32),
        "ident_f": ([128, 128], F32), "cm1": ([128, 1], F32),
        "c1": ([128, 1], F32), "c2pi": ([128, 1], F32),
        "cm2pi": ([128, 1], F32),
        "ones64_f": ([64, 1], F32), "tril_b": ([128, 128], BF16),
    }
    for k, (shape, dt) in specs.items():
        h = nc.alloc_sbuf_tensor("cs_" + k, shape, dt)
        sb[k] = h.ap()
        toks.append(co.ld.put(sb[k], dC[k]))
    for e in co.engines():
        co.ld.wait_all(e)
    return sb


def host_weights(inputs):
    W = {}
    W["l0f_w1"] = np.ascontiguousarray(np.asarray(inputs["L0_ffn_w_gate"]).T).reshape(DC, 128, 3072)
    W["l0f_wu"] = np.ascontiguousarray(np.asarray(inputs["L0_ffn_w_up"]).T).reshape(DC, 128, 3072)
    W["l0f_w2"] = np.ascontiguousarray(np.asarray(inputs["L0_ffn_w_down"]).T).reshape(24, 128, 1024)
    W["l1f_w1"] = np.ascontiguousarray(np.asarray(inputs["L1_ffn_w1"])).reshape(DC, 128, 4096)
    W["l1f_w2"] = np.ascontiguousarray(np.asarray(inputs["L1_ffn_w2"])).reshape(32, 128, 1024)
    W["l1f_b1"] = _bias_cols(inputs["L1_ffn_b1"])
    W["l1f_b2"] = _bias_cols(inputs["L1_ffn_b2"])
    for nm, k in (("wq", "L1_attn_w_Q"), ("wk", "L1_attn_w_K"), ("wv", "L1_attn_w_V")):
        W[f"l1a_{nm}"] = np.ascontiguousarray(np.asarray(inputs[k])).reshape(DC, 128, DM)
    for nm, k in (("wq", "L2_attn_Wq"), ("wk", "L2_attn_Wk"),
                  ("wv", "L2_attn_Wv"), ("wo", "L2_attn_Wo")):
        W[f"l2a_{nm}"] = np.ascontiguousarray(np.asarray(inputs[k])).reshape(DC, 128, DM)
    W["l2a_bq"] = _bias_cols(inputs["L2_attn_bq"])
    W["l2a_bk"] = _bias_cols(inputs["L2_attn_bk"])
    W["l2a_bo"] = _bias_cols(inputs["L2_attn_bo"])
    W["l2a_bv_row"] = np.ascontiguousarray(np.asarray(inputs["L2_attn_bv"]).reshape(1, DM))
    inv_freq = np.exp(np.arange(0, HD, 2, dtype=np.float32) * (-np.log(10000.0) / HD))
    ang = np.arange(SEQ, dtype=np.float32)[None, :] * inv_freq[:, None]
    cs = np.zeros((128, 2, SEQ), dtype=np.float32)
    for b in range(4):
        cs[b * 32:(b + 1) * 32, 0, :] = np.cos(ang)
        cs[b * 32:(b + 1) * 32, 1, :] = np.sin(ang)
    W["l2a_ropeCS"] = cs
    qkv = np.asarray(inputs["L3_attn_qkv_w"])
    perm = np.zeros(3 * DM, dtype=np.int64)
    for h in range(NH):
        for part in range(3):
            perm[part * DM + h * HD: part * DM + (h + 1) * HD] = \
                np.arange(h * 3 * HD + part * HD, h * 3 * HD + (part + 1) * HD)
    qkvP = np.ascontiguousarray(qkv[:, perm])
    W["l3a_wqkv"] = qkvP.reshape(DC, 128, 3 * DM)
    qkvbP = np.asarray(inputs["L3_attn_qkv_b"])[perm]
    W["l3a_bqkv"] = _bias_cols(qkvbP)
    W["l3a_bv_row"] = np.ascontiguousarray(qkvbP[2 * DM:].reshape(1, DM))
    W["l3a_wo"] = np.ascontiguousarray(np.asarray(inputs["L3_attn_out_w"])).reshape(DC, 128, DM)
    W["l3a_bo"] = _bias_cols(inputs["L3_attn_out_b"])
    inv3 = 1.0 / 10000.0 ** (np.arange(0, HD, 2, dtype=np.float32) / HD)
    c3 = np.zeros((128, DC), dtype=np.float32)
    s3 = np.zeros((128, DC), dtype=np.float32)
    for oc in range(DC):
        for hh in range(2):
            h = 2 * oc + hh
            fr = h * inv3
            col = np.concatenate([np.cos(fr), np.cos(fr)])
            sol = np.concatenate([np.sin(fr), np.sin(fr)])
            c3[hh * 64:(hh + 1) * 64, oc] = col
            s3[hh * 64:(hh + 1) * 64, oc] = sol
    W["l3a_ropeC"] = np.ascontiguousarray(c3)
    W["l3a_ropeS"] = np.ascontiguousarray(s3)
    W["l2f_router"] = np.ascontiguousarray(np.asarray(inputs["L2_ffn_router_w"]).T).reshape(DC, 128, 8)
    W["l2f_gateW"] = np.ascontiguousarray(np.asarray(inputs["L2_ffn_gate_w"]).transpose(0, 2, 1)).reshape(8, DC, 128, 2816)
    W["l2f_upW"] = np.ascontiguousarray(np.asarray(inputs["L2_ffn_up_w"]).transpose(0, 2, 1)).reshape(8, DC, 128, 2816)
    W["l2f_downW"] = np.ascontiguousarray(np.asarray(inputs["L2_ffn_down_w"]).transpose(0, 2, 1)).reshape(8, 22, 128, DM)
    W["l3f_router"] = np.ascontiguousarray(np.asarray(inputs["L3_ffn_gate_w"])).reshape(DC, 128, 8)
    W["l3f_gate_b_row"] = np.ascontiguousarray(np.asarray(inputs["L3_ffn_gate_b"]).reshape(1, 8))
    W["l3f_w1"] = np.ascontiguousarray(np.asarray(inputs["L3_ffn_w1"])).reshape(8, DC, 128, 2048)
    W["l3f_w2"] = np.ascontiguousarray(np.asarray(inputs["L3_ffn_w2"])).reshape(8, DC, 128, 2048)
    W["l3f_wg"] = np.ascontiguousarray(np.asarray(inputs["L3_ffn_wg"])).reshape(8, 16, 128, 2048)
    W["l3f_w3"] = np.ascontiguousarray(np.asarray(inputs["L3_ffn_w3"])).reshape(8, 16, 128, DM)
    W["l3f_b1"] = np.stack([_bias_cols(b) for b in np.asarray(inputs["L3_ffn_b1"])])
    W["l3f_b2"] = np.stack([_bias_cols(b) for b in np.asarray(inputs["L3_ffn_b2"])])
    W["l3f_bg"] = np.stack([_bias_cols(b) for b in np.asarray(inputs["L3_ffn_bg"])])
    W["l3f_b3"] = np.stack([_bias_cols(b) for b in np.asarray(inputs["L3_ffn_b3"])])
    W["l0a_wch"] = np.ascontiguousarray(np.asarray(inputs["L0_attn_W_charge"]).T).reshape(DC, 128, 1)
    W["l0a_bch"] = np.asarray(inputs["L0_attn_b_charge"], dtype=np.float32).reshape(1, 1)
    W["l0a_wsh"] = np.ascontiguousarray(np.asarray(inputs["L0_attn_W_shell"]).T).reshape(DC, 128, 64)
    W["l0a_bsh"] = np.ascontiguousarray(np.asarray(inputs["L0_attn_b_shell"], dtype=np.float32).reshape(64, 1))
    W["l0a_omT"] = np.ascontiguousarray(np.asarray(inputs["L0_attn_omega"]).T)
    W["l0a_rfb"] = np.ascontiguousarray(np.asarray(inputs["L0_attn_rff_bias"], dtype=np.float32).reshape(32, 1))
    wc = np.asarray(inputs["L0_attn_W_combine"])
    W["l0a_wcm"] = np.ascontiguousarray(
        np.stack([wc[:, m * 64:(m + 1) * 64].T for m in range(5)]))
    W["l0a_wv"] = np.ascontiguousarray(np.asarray(inputs["L0_attn_W_v"]).T).reshape(DC, 128, DM)
    W["l0a_bv_row"] = np.ascontiguousarray(np.asarray(inputs["L0_attn_b_v"]).reshape(1, DM))
    W["l0a_wo"] = np.ascontiguousarray(np.asarray(inputs["L0_attn_W_out"]).T).reshape(DC, 128, DM)
    W["l0a_bo"] = _bias_cols(inputs["L0_attn_b_out"])
    W["l0a_log_temp"] = np.asarray(inputs["L0_attn_log_temp"], dtype=np.float32).reshape(1, 1)
    idx = np.arange(SEQ, dtype=np.float32)
    Tfull = np.exp(-0.3 * np.abs(idx[:, None] - idx[None, :]) / 10.0).astype(np.float32)
    W["l0a_Tq"] = np.ascontiguousarray(Tfull.reshape(NQT, 128, SEQ))
    return W


def declare_dram(nc, W, names):
    d = {}
    for n in names:
        a = W[n]
        dt = {np.dtype(np.float32): F32, np.dtype(np.int32): I32}[a.dtype]
        d[n] = nc.dram_tensor(n, list(a.shape), dt, kind="ExternalInput").ap()
    return d


def _ffn_cfg(name, d):
    if name == "l0_ffn":
        return dict(kind="silu_mul", w1=d["l0f_w1"], wu=d["l0f_wu"],
                    w2=d["l0f_w2"], b1=None, b2=None, H=3072, scale=0.5,
                    out_mul=1.0 / (127.0 * 127.0))
    if name == "l1_ffn":
        return dict(kind="gelu", w1=d["l1f_w1"], w2=d["l1f_w2"],
                    b1=d["l1f_b1"], b2=d["l1f_b2"], H=4096, scale=0.5)
    raise KeyError(name)


def _attn_cfg(name, d):
    if name == "l1_attn":
        return dict(kind="l1", causal=False, wq=d["l1a_wq"], wk=d["l1a_wk"],
                    wv=d["l1a_wv"])
    if name == "l2_attn":
        return dict(kind="l2", causal=True, wq=d["l2a_wq"], wk=d["l2a_wk"],
                    wv=d["l2a_wv"], wo=d["l2a_wo"], bq=d["l2a_bq"],
                    bk=d["l2a_bk"], bo=d["l2a_bo"], bv_row=d["l2a_bv_row"],
                    ropeCS=d["l2a_ropeCS"])
    if name == "l3_attn":
        return dict(kind="l3", causal=False, wqkv=d["l3a_wqkv"],
                    bqkv=d["l3a_bqkv"], bv_row=d["l3a_bv_row"],
                    wo=d["l3a_wo"], bo=d["l3a_bo"],
                    ropeC=d["l3a_ropeC"], ropeS=d["l3a_ropeS"])
    raise KeyError(name)


def _attn0_cfg(d):
    coeffs = np.asarray([(-1.0) ** m / math.factorial(m) for m in range(5)],
                        dtype=np.float32)
    mags = np.sqrt(np.abs(coeffs) + 1e-10).astype(np.float32)
    return dict(wch=d["l0a_wch"], bch=d["l0a_bch"], wsh=d["l0a_wsh"],
                bsh=d["l0a_bsh"], omT=d["l0a_omT"], rfb=d["l0a_rfb"],
                wcm=d["l0a_wcm"], wv=d["l0a_wv"], bv_row=d["l0a_bv_row"],
                wo=d["l0a_wo"], bo=d["l0a_bo"], log_temp=d["l0a_log_temp"],
                Tq=d["l0a_Tq"], mags=[float(m) for m in mags])


def _moe_cfg(name, d):
    if name == "l2_ffn":
        return dict(kind="l2", router=d["l2f_router"], gateW=d["l2f_gateW"],
                    upW=d["l2f_upW"], downW=d["l2f_downW"])
    if name == "l3_ffn":
        return dict(kind="l3", router=d["l3f_router"],
                    gate_b_row=d["l3f_gate_b_row"], w1=d["l3f_w1"],
                    w2=d["l3f_w2"], wg=d["l3f_wg"], w3=d["l3f_w3"],
                    b1=d["l3f_b1"], b2=d["l3f_b2"], bg=d["l3f_bg"],
                    b3=d["l3f_b3"])
    raise KeyError(name)


PART_WEIGHTS = {
    "l0_ffn": ["l0f_w1", "l0f_wu", "l0f_w2"],
    "l1_ffn": ["l1f_w1", "l1f_w2", "l1f_b1", "l1f_b2"],
    "l1_attn": ["l1a_wq", "l1a_wk", "l1a_wv"],
    "l2_attn": ["l2a_wq", "l2a_wk", "l2a_wv", "l2a_wo", "l2a_bq", "l2a_bk",
                "l2a_bo", "l2a_bv_row", "l2a_ropeCS"],
    "l3_attn": ["l3a_wqkv", "l3a_bqkv", "l3a_bv_row", "l3a_wo", "l3a_bo",
                "l3a_ropeC", "l3a_ropeS"],
    "l2_ffn": ["l2f_router", "l2f_gateW", "l2f_upW", "l2f_downW"],
    "l0_attn": ["l0a_wch", "l0a_bch", "l0a_wsh", "l0a_bsh", "l0a_omT",
                "l0a_rfb", "l0a_wcm", "l0a_wv", "l0a_bv_row", "l0a_wo",
                "l0a_bo", "l0a_log_temp", "l0a_Tq"],
    "l3_ffn": ["l3f_router", "l3f_gate_b_row", "l3f_w1", "l3f_w2", "l3f_wg",
               "l3f_w3", "l3f_b1", "l3f_b2", "l3f_bg", "l3f_b3"],
}


def emit_part(co, C, name, d, Xin, Xout, scratch):
    if name in ("l0_ffn", "l1_ffn"):
        emit_ffn(co, C, Xin, Xout, scratch["XN"], _ffn_cfg(name, d))
    elif name in ("l1_attn", "l2_attn", "l3_attn"):
        emit_attn(co, C, Xin, Xout, scratch, _attn_cfg(name, d))
    elif name in ("l2_ffn", "l3_ffn"):
        emit_moe(co, C, Xin, Xout, scratch, _moe_cfg(name, d))
    elif name == "l0_attn":
        emit_attn_l0(co, C, Xin, Xout, scratch, _attn0_cfg(d))
    else:
        raise KeyError(name)


def build_program(parts):
    import concourse.nx_utils as _nxu
    _nxu.display_graph = lambda G: None
    nc = bacc.Bacc("TRN2", target_bir_lowering=False)
    consts = make_consts()
    dC = declare_consts(nc, consts)
    wnames = sorted({w for p in parts for w in PART_WEIGHTS[p]})
    Whost = None
    x_in = nc.dram_tensor("x_in", [NSEQ, DC, 128, SEQ], F32, kind="ExternalInput").ap()
    x_out = nc.dram_tensor("x_out", [NSEQ, DC, 128, SEQ], F32, kind="ExternalOutput").ap()
    XS = [nc.dram_tensor(f"xs{i}", [NSEQ, DC, 128, SEQ], F32, kind="Internal").ap()
          for i in range(2)]
    scratch = {
        "XN": nc.dram_tensor("xn_sp", [NSEQ, DC, 128, SEQ], BF16, kind="Internal").ap(),
        "QSP": nc.dram_tensor("q_sp", [DC, 128, SEQ], BF16, kind="Internal").ap(),
        "CTX": nc.dram_tensor("ctx_sp", [DC, 128, SEQ], BF16, kind="Internal").ap(),
        "AACC": nc.dram_tensor("a_sp", [NSEQ, DC, 128, SEQ], F32, kind="Internal").ap(),
        "SSP": nc.dram_tensor("s_sp", [NSEQ, 16, 128, SEQ], BF16, kind="Internal").ap(),
    }
    dW = {}
    for n in wnames:
        shape, dt = DRAM_SHAPES[n]
        dW[n] = nc.dram_tensor(n, list(shape), dt, kind="ExternalInput").ap()
    co = Core(nc)
    C = load_consts(co, dC)
    for k, name in enumerate(parts):
        Xin = x_in if k == 0 else XS[k % 2]
        Xout = x_out if k == len(parts) - 1 else XS[(k + 1) % 2]
        emit_part(co, C, name, dW, Xin, Xout, scratch)
    nc.finalize()
    return nc


DRAM_SHAPES = {
    "l0f_w1": ((DC, 128, 3072), I32), "l0f_wu": ((DC, 128, 3072), I32),
    "l0f_w2": ((24, 128, 1024), I32),
    "l1f_w1": ((DC, 128, 4096), F32), "l1f_w2": ((32, 128, 1024), F32),
    "l1f_b1": ((128, 32), F32), "l1f_b2": ((128, 8), F32),
    "l1a_wq": ((DC, 128, DM), F32), "l1a_wk": ((DC, 128, DM), F32),
    "l1a_wv": ((DC, 128, DM), F32),
    "l2a_wq": ((DC, 128, DM), F32), "l2a_wk": ((DC, 128, DM), F32),
    "l2a_wv": ((DC, 128, DM), F32), "l2a_wo": ((DC, 128, DM), F32),
    "l2a_bq": ((128, 8), F32), "l2a_bk": ((128, 8), F32),
    "l2a_bo": ((128, 8), F32), "l2a_bv_row": ((1, DM), F32),
    "l2a_ropeCS": ((128, 2, SEQ), F32),
    "l3a_wqkv": ((DC, 128, 3 * DM), F32), "l3a_bqkv": ((128, 24), F32),
    "l3a_bv_row": ((1, DM), F32), "l3a_wo": ((DC, 128, DM), F32),
    "l3a_bo": ((128, 8), F32),
    "l3a_ropeC": ((128, DC), F32), "l3a_ropeS": ((128, DC), F32),
    "l2f_router": ((DC, 128, 8), F32),
    "l2f_gateW": ((8, DC, 128, 2816), F32), "l2f_upW": ((8, DC, 128, 2816), F32),
    "l2f_downW": ((8, 22, 128, DM), F32),
    "l3f_router": ((DC, 128, 8), F32), "l3f_gate_b_row": ((1, 8), F32),
    "l3f_w1": ((8, DC, 128, 2048), F32), "l3f_w2": ((8, DC, 128, 2048), F32),
    "l3f_wg": ((8, 16, 128, 2048), F32), "l3f_w3": ((8, 16, 128, DM), F32),
    "l3f_b1": ((8, 128, 16), F32), "l3f_b2": ((8, 128, 16), F32),
    "l3f_bg": ((8, 128, 16), F32), "l3f_b3": ((8, 128, 8), F32),
    "l0a_wch": ((DC, 128, 1), F32), "l0a_bch": ((1, 1), F32),
    "l0a_wsh": ((DC, 128, 64), F32), "l0a_bsh": ((64, 1), F32),
    "l0a_omT": ((64, 32), F32), "l0a_rfb": ((32, 1), F32),
    "l0a_wcm": ((5, 64, 64), F32), "l0a_wv": ((DC, 128, DM), F32),
    "l0a_bv_row": ((1, DM), F32), "l0a_wo": ((DC, 128, DM), F32),
    "l0a_bo": ((128, 8), F32), "l0a_log_temp": ((1, 1), F32),
    "l0a_Tq": ((NQT, 128, SEQ), F32),
}

ALL_PARTS = [
    "l0_ffn", "l0_attn", "l0_ffn",
    "l1_ffn", "l1_attn", "l1_ffn",
    "l2_ffn", "l2_attn", "l2_ffn",
    "l3_ffn", "l3_attn", "l3_ffn",
]

FUSED = True
_PROG_CACHE = {}


def _get_program(parts):
    key = tuple(parts)
    if key not in _PROG_CACHE:
        _PROG_CACHE[key] = build_program(list(parts))
    return _PROG_CACHE[key]


def shard_x(x):
    xt = np.ascontiguousarray(np.asarray(x, dtype=np.float32).transpose(0, 2, 1))
    xt = xt.reshape(16, DC, 128, SEQ)
    return [np.ascontiguousarray(xt[2 * c:2 * c + 2]) for c in range(NCORES)]


def unshard_x(shards):
    full = np.concatenate([s for s in shards], axis=0)
    full = full.reshape(16, DM, SEQ).transpose(0, 2, 1)
    return np.ascontiguousarray(full)


def run_parts(parts_groups, inputs, x):
    consts = make_consts()
    W = host_weights(inputs)
    xs = shard_x(x)
    for parts in parts_groups:
        nc = _get_program(tuple(parts))
        wnames = sorted({w for p in parts for w in PART_WEIGHTS[p]})
        in_maps = []
        for c in range(NCORES):
            m = {"x_in": xs[c]}
            for n in wnames:
                m[n] = W[n]
            for k, a in consts.items():
                m["c_" + k] = np.asarray(a)
            in_maps.append(m)
        res = run_bass_kernel_spmd(nc, in_maps, core_ids=list(range(NCORES)))
        xs = [res.results[c]["x_out"] for c in range(NCORES)]
    return unshard_x(xs)


def kernel(**inputs):
    x = np.asarray(inputs["L0_attn_x"], dtype=np.float32)
    if FUSED:
        groups = [ALL_PARTS]
    else:
        groups = [[p] for p in ALL_PARTS]
    return run_parts(groups, inputs, x)
